# version 37 (final)
# baseline (speedup 1.0000x reference)
import math
import numpy as np

import concourse.bass as bass
import concourse.mybir as mybir
from concourse.ap import AP
from concourse.bass_utils import run_bass_kernel_spmd

F32 = mybir.dt.float32
U16 = mybir.dt.uint16
BF16 = mybir.dt.bfloat16
I32 = mybir.dt.int32
AF = mybir.ActivationFunctionType
ALU = mybir.AluOpType
AX = mybir.AxisListType

T = 2048
D = 1024
NB = 16
NH = 16
DH = 64
E = 4
DI = 4096
NS = 4
ST = 512
LN_EPS = 1e-5
RMS_EPS = 1.1920929e-07
FFN_LN_EPS = 1e-12
EPS_LIN = 1e-6
THETA = 10000.0

ENG = ("sync", "scalar", "vector", "tensor", "gpsimd")


class Rs:
    __slots__ = ("name", "w", "r", "psum")

    def __init__(self, name, psum=False):
        self.name = name
        self.w = None
        self.r = []
        self.psum = psum


class Bld:
    def __init__(self):
        self.prog = {e: [] for e in ENG}
        self.cnt = {}
        self.waited = {e: {} for e in ENG}
        self.drained = {e: 0 for e in ENG}
        for e in ENG:
            self.cnt[e + "c"] = 0
            self.cnt[e + "d"] = 0

    @staticmethod
    def tok_eng(tok):
        s = tok[0]
        return s.split("d_")[0] if "d_" in s else s[:-1]

    def emit(self, eng, fn, reads=(), writes=(), dma=None, no_selfwait=False):
        sem = (eng + "d_" + dma) if dma else (eng + "c")
        if dma and sem not in self.cnt:
            self.cnt[sem] = 0
        deps = {}

        def add(tok):
            if tok is None:
                return
            s, v = tok
            if deps.get(s, 0) < v:
                deps[s] = v

        for rs in reads:
            add(rs.w)
            if rs.psum:
                for t in rs.r:
                    if self.tok_eng(t) != eng:
                        add(t)
        for rs in writes:
            add(rs.w)
            for t in rs.r:
                add(t)

        own = eng + "c"
        need_drain = False
        for s, v in sorted(deps.items()):
            if s == own:
                if eng != "tensor" and v > self.drained[eng]:
                    need_drain = True
            else:
                if self.waited[eng].get(s, 0) < v:
                    self.prog[eng].append(("wait", s, v))
                    self.waited[eng][s] = v
        if need_drain:
            self.prog[eng].append(("drain",))
            self.drained[eng] = self.cnt[own]

        if dma and not no_selfwait and self.cnt[sem] > 0 and self.waited[eng].get(sem, 0) < self.cnt[sem]:
            self.prog[eng].append(("wait", sem, self.cnt[sem]))
            self.waited[eng][sem] = self.cnt[sem]

        inc = 16 if dma else 1
        self.cnt[sem] += inc
        tok = (sem, self.cnt[sem])
        self.prog[eng].append(("op", fn, sem, inc))
        for rs in reads:
            rs.r.append(tok)
        for rs in writes:
            rs.w = tok
            rs.r = []
        return tok


def build_nc():
    nc = bass.Bass("TRN2", target_bir_lowering=False, debug=False)
    b = Bld()
    em = b.emit

    x_d = nc.dram_tensor("x", [T, D], F32, kind="ExternalInput").ap()
    wqkvT_d = nc.dram_tensor("wqkvT", [D, 3 * D], F32, kind="ExternalInput").ap()
    bqkv_d = nc.dram_tensor("bqkv", [1, 3 * D], F32, kind="ExternalInput").ap()
    gq_d = nc.dram_tensor("gq", [1, D], F32, kind="ExternalInput").ap()
    gk_d = nc.dram_tensor("gk", [1, D], F32, kind="ExternalInput").ap()
    wr_d = nc.dram_tensor("wr", [D, E], F32, kind="ExternalInput").ap()
    woutT_d = nc.dram_tensor("woutT", [D, D], F32, kind="ExternalInput").ap()
    bout_d = nc.dram_tensor("bout", [1, D], F32, kind="ExternalInput").ap()
    gout_d = nc.dram_tensor("gout", [1, D], F32, kind="ExternalInput").ap()
    w1_d = nc.dram_tensor("w1", [E * 32, 128, 1024], F32, kind="ExternalInput").ap()
    b1t_d = nc.dram_tensor("b1t", [E, 128, 32], F32, kind="ExternalInput").ap()
    wo_d = nc.dram_tensor("wo", [DI, D], F32, kind="ExternalInput").ap()
    bo_d = nc.dram_tensor("bo", [1, D], F32, kind="ExternalInput").ap()
    gamma_d = nc.dram_tensor("gamma", [1, D], F32, kind="ExternalInput").ap()
    beta_d = nc.dram_tensor("beta", [1, D], F32, kind="ExternalInput").ap()
    out_d = nc.dram_tensor("out", [T, D], F32, kind="ExternalOutput").ap()
    yscr_d = nc.dram_tensor("y_scr", [T, D], BF16, kind="Internal").ap()
    wo_bf_d = nc.dram_tensor("wo_bf", [DI, D], BF16, kind="Internal").ap()

    import contextlib
    ctx = contextlib.ExitStack()
    EN = ctx.enter_context

    def sb(name, shape, dtype):
        t = EN(nc.sbuf_tensor(name, shape, dtype))
        return t, Rs(name)

    def ps(name):
        t = EN(nc.psum_tensor(name, [128, 512], F32))
        return t, Rs(name, psum=True)

    ident, r_ident = sb("ident", [128, 128], F32)
    identb, r_identb = sb("identb", [128, 128], BF16)
    ones1, r_ones1 = sb("ones1", [1, 128], BF16)
    selheads, r_selh = sb("selheads", [16, 1024], BF16)
    erows, r_erows = sb("erows", [4, 512], BF16)
    onescol, r_onescol = sb("onescol", [128, 1], BF16)
    cos_all, r_cos = sb("cos_all", [128, NB * 32], BF16)
    spair, r_spair = sb("spair", [128, NB * 64], BF16)
    gqb, r_gqb = sb("gqb", [128, D], BF16)
    gkb, r_gkb = sb("gkb", [128, D], BF16)
    bqb, r_bqb = sb("bqb", [128, 3 * D], BF16)
    boutb, r_boutb = sb("boutb", [128, D], BF16)
    bob, r_bob = sb("bob", [128, D], BF16)
    gammab, r_gammab = sb("gammab", [128, D], BF16)
    betab, r_betab = sb("betab", [128, D], BF16)
    goutb, r_goutb = sb("goutb", [128, D], BF16)
    b1t_sb, r_b1t = sb("b1t_sb", [128, 128], F32)
    wrh, r_wrh = sb("wrh", [128, 32], BF16)
    wrl, r_wrl = sb("wrl", [128, 32], BF16)
    lnmu, r_lnmu = sb("lnmu", [128, NB], F32)
    lnis, r_lnis = sb("lnis", [128, NB], F32)
    mask_tm, r_mask_tm = sb("mask_tm", [128, 4 * NB], F32)
    xn_fm, r_xn = sb("xn_fm", [128, 8 * T], BF16)
    qf_fm, r_qf = sb("qf_fm", [128, 8 * T], BF16)
    vkN, r_vkN = sb("vkN", [128, 1024], BF16)
    sfull, r_sfull = sb("sfull", [128, 128], BF16)
    woutT_sb, r_wout = sb("woutT_sb", [128, 8 * D], BF16)

    AR_COLS = 24576
    arena, _ = sb("arena", [128, AR_COLS], BF16)
    r_wq, r_wk, r_wv = Rs("wq"), Rs("wk"), Rs("wv")
    AQ, AK, AV = 0, 8192, 16384
    AW1 = [0, 2048]
    AWO = [4096, 5120]
    AI = 6144
    r_w1b = [Rs("w1b0"), Rs("w1b1")]
    r_wob = [Rs("wob0"), Rs("wob1")]
    r_int = [Rs(f"int{i}") for i in range(32)]

    def ar(c0, n):
        return arena[:, c0:c0 + n]

    scrF, _ = sb("scrF", [128, 4096], F32)
    scrB, _ = sb("scrB", [128, 12288], BF16)
    rF = [Rs(f"F{i}") for i in range(4)]

    def f_(i):
        return scrF[:, 1024 * i:1024 * i + 1024]

    xn_t = scrF[:, 3072:4096]
    t1_t = f_(0)
    q_t = f_(1)
    t2_t = f_(2)
    iof1 = scrF[:, 3072:3200]
    rop1 = scrF[:, 3200:3232]
    rop3 = scrF[:, 3232:3264]
    wr_f = scrF[:, 3264:3296]
    ropg = scrF[:, 3296:3328]
    r_xn_t = rF[3]
    r_t1 = rF[0]
    r_q_t = rF[1]
    r_t2 = rF[2]
    r_iof1 = r_rop1 = r_rop3 = r_wr_f = r_ropg = rF[3]

    xl_t = scrB[:, 0:1024]
    kf_t = [scrB[:, 1024:2048], scrB[:, 2048:3072]]
    v_t = [scrB[:, 3072:4096], scrB[:, 4096:5120]]
    rowbf = scrB[0:1, 9216:9216 + 3 * D]
    attn_sb = scrB[:, 0:4096]
    maskb = scrB[:, 4096:6144]
    qfp_sb = scrB[:, 6144:6656]
    g_sb = [scrB[:, 6656:7168], scrB[:, 6656:7168]]
    mac_t = scrB[:, 6144:6656]
    den_f = scrB[0:16, 7168:7680]
    rd_bf = scrB[0:16, 4096:6144]
    mrows = scrB[0:4, 7168:7680]
    y_sb = [scrB[:, 8192 + 1024 * i:8192 + 1024 * (i + 1)] for i in range(4)]
    r_xl = Rs("xl")
    r_xlB = Rs("xlB")
    r_kf = [Rs("kf0"), Rs("kf1")]
    r_v = [Rs("v0"), Rs("v1")]
    r_rowbf = Rs("rowbf")
    r_attn = Rs("attn")
    r_maskb = Rs("maskb")
    r_qfp = Rs("qfp")
    _rg = Rs("g")
    r_g = [_rg, _rg]
    r_mac = r_qfp
    mac2 = scrB[:, 1024:1536]
    r_mac2 = Rs("mac2")
    g_x2 = scrB[:, 1536:2048]
    r_gx2 = Rs("gx2")
    g_x3 = scrB[:, 7168:7680]
    r_gx3 = Rs("gx3")
    r_den = Rs("den")
    r_rd = Rs("rd")
    r_mrows = r_den
    r_y = [Rs(f"y{i}") for i in range(4)]

    stc, r_stc = sb("stc", [128, 40], F32)
    lg_f, r_lg_f = sb("lg_f", [4, 128], F32)
    lg_tm, r_lg_tm = sb("lg_tm", [128, 8], F32)
    eq_t, r_eq_t = sb("eq_t", [128, 4], F32)
    alr_t, r_alr_t = sb("alr_t", [128, 4], F32)
    iot1, r_iot1 = sb("iot1", [128, 128], I32)
    iot2, r_iot2 = sb("iot2", [128, 1], I32)
    pcol, r_pcol = sb("pcol", [128, 1], F32)
    invf2, r_invf2 = sb("invf2", [128, 32], F32)
    rop2, r_rop2 = sb("rop2", [128, 32], I32)
    x_tm = [None, None]
    r_x_tm = [None, None]
    x_tm[0], r_x_tm[0] = sb("x_tm0", [128, D], F32)
    x_tm[1], r_x_tm[1] = sb("x_tm1", [128, D], F32)
    x_tm.append(arena[:, 20480:22528].bitcast(F32)[:, 0:1024])
    r_x_tm.append(Rs("x_tm2"))

    pP, r_pP = [], []
    for i in range(4):
        t_, rr_ = ps(f"pP{i}")
        pP.append(t_)
        r_pP.append(rr_)
    pAux, r_pAux = ps("pAux")
    pVK, r_pVK = [], []
    for i in range(3):
        t_, rr_ = ps(f"pVK{i}")
        pVK.append(t_)
        r_pVK.append(rr_)

    def xn_s(c8, col0, n):
        return xn_fm[:, 2048 * c8 + col0: 2048 * c8 + col0 + n]

    def qf_s(c8, col0, n):
        return qf_fm[:, 2048 * c8 + col0: 2048 * c8 + col0 + n]

    def alias(new_rs, old_list):
        toks = []
        for o in old_list:
            if o.w:
                toks.append(o.w)
            toks.extend(o.r)
        new_rs.r = list(toks)
        new_rs.w = None

    em("gpsimd", lambda e: e.iota(iot1[:], pattern=[[1, 128]], base=0, channel_multiplier=0),
       writes=[r_iot1])
    em("gpsimd", lambda e: e.iota(iot2[:], pattern=[[0, 1]], base=0, channel_multiplier=1),
       writes=[r_iot2])
    em("vector", lambda e: e.tensor_copy(iof1, iot1[:]), reads=[r_iot1], writes=[r_iof1])
    em("vector", lambda e: e.tensor_copy(pcol[:], iot2[:]), reads=[r_iot2], writes=[r_pcol])
    em("vector", lambda e: e.tensor_scalar(ident[:], iof1, pcol[:], None, ALU.is_equal),
       reads=[r_iof1, r_pcol], writes=[r_ident])
    em("vector", lambda e: e.tensor_copy(identb[:], ident[:]), reads=[r_ident], writes=[r_identb])
    em("vector", lambda e: e.memset(ones1[:], 1.0), writes=[r_ones1])
    em("vector", lambda e: e.memset(onescol[:], 1.0), writes=[r_onescol])
    em("gpsimd", lambda e: e.iota(selheads[:], pattern=[[-1, 16], [0, 64]], base=0,
                                  channel_multiplier=1, allow_small_or_imprecise_dtypes=True),
       writes=[r_selh])
    em("vector", lambda e: e.tensor_single_scalar(selheads[:], selheads[:], 0.0, ALU.is_equal),
       reads=[r_selh], writes=[r_selh])
    em("gpsimd", lambda e: e.iota(erows[:], pattern=[[-1, 4], [0, 128]], base=0,
                                  channel_multiplier=1, allow_small_or_imprecise_dtypes=True),
       writes=[r_erows])
    em("vector", lambda e: e.tensor_single_scalar(erows[:], erows[:], 0.0, ALU.is_equal),
       reads=[r_erows], writes=[r_erows])
    em("vector", lambda e: e.memset(vkN[:], 0.0), writes=[r_vkN])
    em("vector", lambda e: e.memset(sfull[:], 0.0), writes=[r_sfull])

    em("gpsimd", lambda e: e.dma_start(
        out=ar(AQ, 8192), in_=AP(wqkvT_d.tensor, 0, [[3072, 128], [393216, 8], [1, 1024]])),
       writes=[r_wq], dma="wq")


    def rsqrt_steps():
        ti = stc[:, 4:5].bitcast(I32)
        yi = stc[:, 9:10].bitcast(I32)
        em("vector", lambda e: e.tensor_single_scalar(yi, ti, 1, ALU.arith_shift_right),
           reads=[r_stc], writes=[r_stc])
        em("vector", lambda e: e.tensor_scalar(yi, yi, -1, 0x5f3759df, ALU.mult, ALU.add),
           reads=[r_stc], writes=[r_stc])
        for _ in range(3):
            em("vector", lambda e: e.tensor_mul(stc[:, 7:8], stc[:, 9:10], stc[:, 9:10]),
               reads=[r_stc], writes=[r_stc])
            em("vector", lambda e: e.tensor_scalar(stc[:, 7:8], stc[:, 7:8], stc[:, 4:5], -0.5,
                                                   ALU.mult, ALU.mult),
               reads=[r_stc], writes=[r_stc])
            em("vector", lambda e: e.tensor_scalar_add(stc[:, 7:8], stc[:, 7:8], 1.5),
               reads=[r_stc], writes=[r_stc])
            em("vector", lambda e: e.tensor_mul(stc[:, 9:10], stc[:, 9:10], stc[:, 7:8]),
               reads=[r_stc], writes=[r_stc])

    def rsqrt_batch(wd=4):
        ti = stc[:, 4:4 + wd].bitcast(I32)
        yi = stc[:, 24:24 + wd].bitcast(I32)
        em("vector", lambda e: e.tensor_single_scalar(yi, ti, 1, ALU.arith_shift_right),
           reads=[r_stc], writes=[r_stc])
        em("vector", lambda e: e.tensor_scalar(yi, yi, -1, 0x5f3759df, ALU.mult, ALU.add),
           reads=[r_stc], writes=[r_stc])
        for _ in range(3):
            em("vector", lambda e: e.tensor_mul(stc[:, 20:20 + wd], stc[:, 24:24 + wd],
                                                stc[:, 24:24 + wd]),
               reads=[r_stc], writes=[r_stc])
            em("vector", lambda e: e.scalar_tensor_tensor(
                stc[:, 20:20 + wd], stc[:, 20:20 + wd], -0.5, stc[:, 4:4 + wd],
                ALU.mult, ALU.mult),
               reads=[r_stc], writes=[r_stc])
            em("vector", lambda e: e.tensor_scalar_add(stc[:, 20:20 + wd], stc[:, 20:20 + wd], 1.5),
               reads=[r_stc], writes=[r_stc])
            em("vector", lambda e: e.tensor_mul(stc[:, 24:24 + wd], stc[:, 24:24 + wd],
                                                stc[:, 20:20 + wd]),
               reads=[r_stc], writes=[r_stc])

    globals()
    AW1S = [8192 + 1024 * i for i in range(12)]
    r_w1s = [Rs(f"w1s{i}") for i in range(12)]
    AWO[0], AWO[1] = 20480, 21504
    AIP = [22528, 23040]
    r_intp = [Rs("intp0"), Rs("intp1")]
    int_scr = nc.dram_tensor("int_scr", [32, 128, 2048], BF16, kind="Internal").ap()
    r_iscr = [[Rs(f"iscr{i}_{s}") for s in range(4)] for i in range(32)]
    w1scr = nc.dram_tensor("w1_scr", [128, 128, 1024], BF16, kind="Internal").ap()
    r_w1all = Rs("w1all")
    r_yscr = [Rs(f"yscr{i}") for i in range(NB)]


    def emit_a0x(blk):
        p3 = blk % 3
        xt, r_xt = x_tm[p3], r_x_tm[p3]
        em("sync", lambda e, blk=blk, xt=xt: e.dma_start(
            out=xt if xt.__class__.__name__ == "AP" else xt[:],
            in_=x_d[128 * blk:128 * blk + 128, :]),
           writes=[r_xt], dma="x%d" % p3)

    r_fsum = [Rs("fsum0"), Rs("fsum1")]
    r_fsq = [Rs("fsq0"), Rs("fsq1")]
    xl2 = [scrB[:, 0:1024], scrB[:, 1024:2048]]
    r_xl2 = [r_xl, r_xlB]

    def emit_a0a_sums(blk):
        par = blk % 2
        p3 = blk % 3
        xt, r_xt = x_tm[p3], r_x_tm[p3]
        em("scalar", lambda e, xt=xt, par=par: e.activation(t1_t, xt[:], AF.Identity,
                                                            accum_out=stc[:, 28 + par:29 + par]),
           reads=[r_xt], writes=[r_t1, r_fsum[par]])
        em("scalar", lambda e, xt=xt, par=par: e.activation(t2_t, xt[:], AF.Square,
                                                            accum_out=stc[:, 16 + par:17 + par]),
           reads=[r_xt], writes=[r_t2, r_fsq[par]])

    def emit_a0a_fin(b0):
        em("vector", lambda e: e.tensor_scalar_mul(stc[:, 32:34], stc[:, 28:30], 1.0 / D),
           reads=[r_fsum[0], r_fsum[1]], writes=[r_stc])
        em("vector", lambda e: e.tensor_scalar(stc[:, 4:6], stc[:, 16:18], 1.0 / D, LN_EPS,
                                               ALU.mult, ALU.add),
           reads=[r_fsq[0], r_fsq[1]], writes=[r_stc])
        em("vector", lambda e: e.tensor_mul(stc[:, 20:22], stc[:, 32:34], stc[:, 32:34]),
           reads=[r_stc], writes=[r_stc])
        em("vector", lambda e: e.tensor_sub(stc[:, 4:6], stc[:, 4:6], stc[:, 20:22]),
           reads=[r_stc], writes=[r_stc])
        rsqrt_batch(2)
        em("vector", lambda e, b0=b0: e.tensor_copy(lnmu[:, b0:b0 + 2], stc[:, 32:34]),
           reads=[r_stc], writes=[r_lnmu])
        em("vector", lambda e, b0=b0: e.tensor_copy(lnis[:, b0:b0 + 2], stc[:, 24:26]),
           reads=[r_stc], writes=[r_lnis])
        em("vector", lambda e: e.scalar_tensor_tensor(
            stc[:, 36:38], stc[:, 32:34], -1.0, stc[:, 24:26], ALU.mult, ALU.mult),
           reads=[r_stc], writes=[r_stc])

    def emit_a0ap(blk, xb, r_xb):
        par = blk % 2
        p3 = blk % 3
        xt, r_xt = x_tm[p3], r_x_tm[p3]
        em("scalar", lambda e, xt=xt, par=par, xb=xb: e.activation(
            xb, xt[:], AF.Identity, scale=stc[:, 24 + par:25 + par],
            bias=stc[:, 36 + par:37 + par]),
           reads=[r_xt, r_stc], writes=[r_xb])

    def emit_a0b(blk, xb, r_xb):
        par = blk % 2
        xlp, r_xlp = xl2[par], r_xl2[par]
        pt = (pP[2], pP[3]) if par == 0 else (pP[0], pP[1])
        r_pt = (r_pP[2], r_pP[3]) if par == 0 else (r_pP[0], r_pP[1])
        for c in range(8):
            tgt, r_tgt = (pt[0], r_pt[0]) if c < 4 else (pt[1], r_pt[1])
            em("tensor", lambda e, c=c, tgt=tgt, xb=xb: e.transpose(
                tgt[:, 128 * (c % 4):128 * (c % 4) + 128], xb[:, 128 * c:128 * c + 128], ident[:]),
               reads=[r_xb, r_ident], writes=[r_tgt])
        for half in range(2):
            src, r_src = (pt[0], r_pt[0]) if half == 0 else (pt[1], r_pt[1])
            dst = AP(xn_fm[:].tensor, 128 * blk + 2048 * 4 * half,
                     [[8 * T, 128], [2048, 4], [1, 128]])
            em("vector", lambda e, src=src, dst=dst: e.tensor_copy(dst, src[:]),
               reads=[r_src], writes=[r_xn])
            xnsl = AP(xn_fm[:].tensor, 128 * blk + 2048 * 4 * half,
                      [[8 * T, 128], [2048, 4], [1, 128]])
            em("vector", lambda e, src=src, xnsl=xnsl, half=half, xlp=xlp: e.tensor_sub(
                xlp[:, 512 * half:512 * half + 512], src[:], xnsl),
               reads=[r_src, r_xn], writes=[r_xlp])

    def emit_a0c_mm(blk, half):
        par = blk % 2
        xlp, r_xlp = xl2[par], r_xl2[par]
        for grp, (wt, r_wt, mov_xl) in enumerate(((wrh, r_wrh, False), (wrl, r_wrl, False),
                                                  (wrh, r_wrh, True))):
            for c8 in range(8):
                mv = xlp[:, 128 * c8:128 * c8 + 128] if mov_xl else xn_s(c8, 128 * blk, 128)
                r_mv = r_xlp if mov_xl else r_xn
                em("tensor", lambda e, c8=c8, wt=wt, mv=mv, grp=grp, half=half: e.matmul(
                    pAux[0:4, 128 * half:128 * half + 128], wt[:, 4 * c8:4 * c8 + 4], mv,
                    start=(grp == 0 and c8 == 0), stop=(grp == 2 and c8 == 7)),
                   reads=[r_wt, r_mv], writes=[r_pAux])

    def emit_a0c_cas(b0):
        for half in range(2):
            em("vector", lambda e, half=half: e.tensor_copy(
                lg_f[:], pAux[0:4, 128 * half:128 * half + 128]),
               reads=[r_pAux], writes=[r_lg_f])
            em("tensor", lambda e, half=half: e.transpose(
                pAux[:, 256 + 4 * half:260 + 4 * half], lg_f[:],
                ident[0:4, 0:4]),
               reads=[r_lg_f, r_ident], writes=[r_pAux])
            em("vector", lambda e, half=half: e.tensor_copy(
                lg_tm[:, 4 * half:4 * half + 4], pAux[:, 256 + 4 * half:260 + 4 * half]),
               reads=[r_pAux], writes=[r_lg_tm])
            em("vector", lambda e, half=half: e.reduce_max(
                stc[:, 12 + half:13 + half], lg_tm[:, 4 * half:4 * half + 4], axis=AX.X),
               reads=[r_lg_tm], writes=[r_stc])
            em("vector", lambda e, half=half: e.tensor_scalar(
                lg_tm[:, 4 * half:4 * half + 4], lg_tm[:, 4 * half:4 * half + 4],
                stc[:, 12 + half:13 + half], None, ALU.is_ge),
               reads=[r_stc, r_lg_tm], writes=[r_lg_tm])

        def eqp(j):
            return AP(lg_tm[:].tensor, j, [[8, 128], [4, 2]])

        def mp(j):
            return AP(mask_tm[:].tensor, 4 * b0 + j, [[4 * NB, 128], [4, 2]])

        em("vector", lambda e: e.tensor_copy(mp(0), eqp(0)),
           reads=[r_lg_tm], writes=[r_mask_tm])
        em("vector", lambda e: e.tensor_copy(alr_t[:, 0:2], eqp(0)),
           reads=[r_lg_tm], writes=[r_alr_t])
        for j in (1, 2, 3):
            em("vector", lambda e, j=j: e.tensor_tensor(
                alr_t[:, 2:4], eqp(j), alr_t[:, 0:2], ALU.mult),
               reads=[r_lg_tm, r_alr_t], writes=[r_alr_t])
            em("vector", lambda e, j=j: e.tensor_sub(mp(j), eqp(j), alr_t[:, 2:4]),
               reads=[r_lg_tm, r_alr_t], writes=[r_mask_tm])
            if j < 3:
                em("vector", lambda e, j=j: e.tensor_add(alr_t[:, 0:2], alr_t[:, 0:2], mp(j)),
                   reads=[r_mask_tm, r_alr_t], writes=[r_alr_t])

    qb16 = q_t.bitcast(BF16)[:, 0:1024]
    t1b16 = t1_t.bitcast(BF16)[:, 0:1024]
    t2b16 = t2_t.bitcast(BF16)[:, 0:1024]

    def piece_pipeline(blk, pbank0, pbank1, r_pb0, r_pb1, bias_off, gb, r_gb, dst_ap, r_dst):
        for half, (pbk, r_pbk) in enumerate(((pbank0, r_pb0), (pbank1, r_pb1))):
            em("vector", lambda e, h=half, pbk=pbk: e.tensor_add(
                qb16[:, 512 * h:512 * h + 512], pbk[:],
                bqb[:, bias_off + 512 * h: bias_off + 512 * h + 512]),
               reads=[r_pbk, r_bqb], writes=[r_q_t])
        em("scalar", lambda e: e.activation(t2_t, qb16, AF.Square, accum_out=stc[:, 1:2]),
           reads=[r_q_t], writes=[r_t2, r_stc])
        em("vector", lambda e: e.tensor_scalar(stc[:, 4:5], stc[:, 1:2], 1.0 / D, RMS_EPS,
                                               ALU.mult, ALU.add),
           reads=[r_stc], writes=[r_stc])
        rsqrt_steps()
        em("vector", lambda e, gb=gb: e.tensor_mul(qb16, qb16, gb[:]),
           reads=[r_q_t, r_gb], writes=[r_q_t])
        crep = AP(cos_all[:].tensor, 32 * blk, [[NB * 32, 128], [0, 16], [0, 2], [1, 32]])
        srep = AP(spair[:].tensor, 64 * blk, [[NB * 64, 128], [0, 16], [1, 64]])
        qswap = AP(qb16.tensor, qb16.offset + 32, [[8192, 128], [64, 16], [-32, 2], [1, 32]])
        em("vector", lambda e, crep=crep: e.tensor_tensor(t1b16, qb16, crep, ALU.mult),
           reads=[r_q_t, r_cos], writes=[r_t1])
        em("vector", lambda e, qswap=qswap, srep=srep: e.tensor_tensor(t2b16, qswap, srep, ALU.mult),
           reads=[r_q_t, r_spair], writes=[r_t2])
        em("vector", lambda e: e.tensor_add(t1b16, t1b16, t2b16),
           reads=[r_t1, r_t2], writes=[r_t1])
        em("vector", lambda e, dst_ap=dst_ap: e.tensor_scalar(
            dst_ap, t1b16, stc[:, 9:10], 0.0, ALU.mult, ALU.max),
           reads=[r_t1, r_stc], writes=[r_dst])

    def emit_a1a(blk):
        for c8 in range(8):
            for half in range(2):
                tgt, r_tgt = (pP[2], r_pP[2]) if half == 0 else (pP[3], r_pP[3])
                em("tensor", lambda e, c8=c8, blk=blk, h=half, tgt=tgt: e.matmul(
                    tgt[:], xn_s(c8, 128 * blk, 128), ar(AQ + 1024 * c8 + 512 * h, 512),
                    start=(c8 == 0), stop=(c8 == 7)),
                   reads=[r_xn, r_wq], writes=[r_tgt])
        piece_pipeline(blk, pP[2], pP[3], r_pP[2], r_pP[3], 0, gqb, r_gqb, t2_t, r_t2)

    def emit_a1b(blk):
        for wave in range(2):
            tgt, r_tgt = (pP[2], r_pP[2]) if wave == 0 else (pP[3], r_pP[3])
            for c in range(4):
                cc = 4 * wave + c
                em("tensor", lambda e, cc=cc, c=c, tgt=tgt: e.transpose(
                    tgt[:, 128 * c:128 * c + 128], t2_t[:, 128 * cc:128 * cc + 128], ident[:]),
                   reads=[r_t2, r_ident], writes=[r_tgt])
            dst = AP(qf_fm[:].tensor, 128 * blk + 2048 * 4 * wave,
                     [[8 * T, 128], [2048, 4], [1, 128]])
            em("vector", lambda e, dst=dst, tgt=tgt: e.tensor_copy(dst, tgt[:]),
               reads=[r_tgt], writes=[r_qf])

    xb316 = xn_t.bitcast(BF16)[:, 0:1024]

    def a2_kmms(blk, pa, pb_, r_pa, r_pb):
        for c8 in range(8):
            for half, (tgt, r_tgt) in enumerate(((pa, r_pa), (pb_, r_pb))):
                em("tensor", lambda e, c8=c8, blk=blk, h=half, tgt=tgt: e.matmul(
                    tgt[:], xn_s(c8, 128 * blk, 128), ar(AK + 1024 * c8 + 512 * h, 512),
                    start=(c8 == 0), stop=(c8 == 7)),
                   reads=[r_xn, r_wk], writes=[r_tgt])

    def piece_sums(pb0, pb1, r0, r1, qv, r_qv, par):
        for half, (pbk, r_pbk) in enumerate(((pb0, r0), (pb1, r1))):
            em("vector", lambda e, h=half, pbk=pbk, qv=qv: e.tensor_add(
                qv[:, 512 * h:512 * h + 512], pbk[:],
                bqb[:, D + 512 * h: D + 512 * h + 512]),
               reads=[r_pbk, r_bqb], writes=[r_qv])
        em("scalar", lambda e, qv=qv, par=par: e.activation(t2_t, qv, AF.Square,
                                                            accum_out=stc[:, 16 + par:17 + par]),
           reads=[r_qv], writes=[r_t2, r_fsq[par]])

    def piece_batfin():
        em("vector", lambda e: e.tensor_scalar(stc[:, 4:6], stc[:, 16:18], 1.0 / D, RMS_EPS,
                                               ALU.mult, ALU.add),
           reads=[r_fsq[0], r_fsq[1]], writes=[r_stc])
        rsqrt_batch(2)

    def piece_fin(blk, qv, r_qv, dst_ap, r_dst, par):
        em("vector", lambda e, qv=qv: e.tensor_mul(qv, qv, gkb[:]),
           reads=[r_qv, r_gkb], writes=[r_qv])
        crep = AP(cos_all[:].tensor, 32 * blk, [[NB * 32, 128], [0, 16], [0, 2], [1, 32]])
        srep = AP(spair[:].tensor, 64 * blk, [[NB * 64, 128], [0, 16], [1, 64]])
        qswap = AP(qv.tensor, qv.offset + 32, [[8192, 128], [64, 16], [-32, 2], [1, 32]])
        em("vector", lambda e, crep=crep, qv=qv: e.tensor_tensor(t1b16, qv, crep, ALU.mult),
           reads=[r_qv, r_cos], writes=[r_t1])
        em("vector", lambda e, qswap=qswap, srep=srep: e.tensor_tensor(t2b16, qswap, srep, ALU.mult),
           reads=[r_qv, r_spair], writes=[r_t2])
        em("vector", lambda e: e.tensor_add(t1b16, t1b16, t2b16),
           reads=[r_t1, r_t2], writes=[r_t1])
        em("vector", lambda e, dst_ap=dst_ap, par=par: e.tensor_scalar(
            dst_ap, t1b16, stc[:, 24 + par:25 + par], 0.0, ALU.mult, ALU.max),
           reads=[r_t1, r_stc], writes=[r_dst])

    def a2_vmms_evac(blk, pc, pd_, r_pc, r_pd, par):
        for c8 in range(8):
            for half, (tgt, r_tgt) in enumerate(((pc, r_pc), (pd_, r_pd))):
                em("tensor", lambda e, c8=c8, blk=blk, h=half, tgt=tgt: e.matmul(
                    tgt[:], xn_s(c8, 128 * blk, 128), ar(AV + 1024 * c8 + 512 * h, 512),
                    start=(c8 == 0), stop=(c8 == 7)),
                   reads=[r_xn, r_wv], writes=[r_tgt])
        vt, r_vtr = v_t[par], r_v[par]
        for half, (pbk, r_pbk) in enumerate(((pc, r_pc), (pd_, r_pd))):
            em("vector", lambda e, h=half, pbk=pbk, vt=vt: e.tensor_add(
                vt[:, 512 * h:512 * h + 512], pbk[:],
                bqb[:, 2 * D + 512 * h: 2 * D + 512 * h + 512]),
               reads=[r_pbk, r_bqb], writes=[r_vtr])

    def emit_a2_vk(blk):
        par = blk % 2
        kf, r_kfr = kf_t[par], r_kf[par]
        vt, r_vtr = v_t[par], r_v[par]
        for p in range(8):
            bank = p // 3 if p < 6 else 2
            off = (p % 3) * 129 if p < 6 else (p - 6) * 129
            pv, r_pv = pVK[bank], r_pVK[bank]
            first = (blk == 0 and (p % 3 == 0 if p < 6 else p == 6))
            last = (blk == NB - 1 and p in (2, 5, 7))
            em("tensor", lambda e, p=p, off=off, pv=pv, kf=kf, vt=vt, first=first: e.matmul(
                pv[:, off:off + 64], kf[:, 128 * p:128 * p + 128], vt[:, 128 * p:128 * p + 64],
                start=first, stop=False),
               reads=[r_kfr, r_vtr], writes=[r_pv])
            em("tensor", lambda e, p=p, off=off, pv=pv, kf=kf: e.matmul(
                pv[:, off + 64:off + 65], kf[:, 128 * p:128 * p + 128], onescol[:],
                start=False, stop=False),
               reads=[r_kfr, r_onescol], writes=[r_pv])
            em("tensor", lambda e, p=p, off=off, pv=pv, kf=kf, vt=vt, last=last: e.matmul(
                pv[:, off + 65:off + 129], kf[:, 128 * p:128 * p + 128],
                vt[:, 128 * p + 64:128 * p + 128],
                start=False, stop=last),
               reads=[r_kfr, r_vtr], writes=[r_pv])

    M16_OFF = [0, 512, 2048, 2560, 3072, 3584, 4096, 4608,
               5120, 5632, 7680, 8192, 8704, 9216, 9728, 10240]
    r_m16 = [Rs(f"m16_{i}") for i in range(16)]

    def m16_ap(i):
        return scrB[:, M16_OFF[i]:M16_OFF[i] + 512]

    def emit_masks():
        alias(r_m16[0], [r_xl])
        alias(r_m16[1], [r_xl])
        alias(r_m16[2], [r_kf[1]])
        alias(r_m16[3], [r_kf[1]])
        alias(r_m16[4], [r_v[0]])
        alias(r_m16[5], [r_v[0]])
        alias(r_m16[6], [r_v[1], r_maskb])
        alias(r_m16[7], [r_v[1], r_maskb])
        alias(r_m16[8], [r_maskb])
        alias(r_m16[9], [r_maskb])
        alias(r_m16[11], [r_y[0]])
        alias(r_m16[12], [r_y[0]])
        alias(r_m16[13], [r_y[1], r_rowbf])
        alias(r_m16[14], [r_y[1], r_rowbf])
        alias(r_m16[15], [r_y[2], r_rowbf])
        alias(r_mrows, [r_mrows, r_gx3])
        for s in range(NS):
            for b4 in range(4):
                blk = 4 * s + b4
                em("tensor", lambda e, blk=blk, b4=b4: e.transpose(
                    pAux[0:4, 128 * b4:128 * b4 + 128], mask_tm[:, 4 * blk:4 * blk + 4], ident[:]),
                   reads=[r_mask_tm, r_ident], writes=[r_pAux])
            em("vector", lambda e: e.tensor_copy(mrows, pAux[0:4, 0:512]),
               reads=[r_pAux], writes=[r_mrows])
            for ex in range(4):
                em("tensor", lambda e, ex=ex: e.matmul(
                    pAux[:], erows[:, 128 * ex:128 * ex + 128], mrows, start=True, stop=True),
                   reads=[r_erows, r_mrows], writes=[r_pAux])
                em("scalar", lambda e, s=s, ex=ex: e.activation(
                    m16_ap(4 * s + ex), pAux[:], AF.Copy),
                   reads=[r_pAux], writes=[r_m16[4 * s + ex]])
        alias(r_gx3, [r_gx3, r_mrows])

    def w1_dma(ich):
        ph3 = ich % 3
        em("gpsimd", lambda e, ich=ich, ph3=ph3: e.dma_start(
            out=ar(8192 + ph3 * 4096, 4096),
            in_=AP(w1_d.tensor, ich * 131072,
                   [[1024, 128], [4194304, 4], [1, 1024]])),
           writes=[r_w1s[ph3]], dma="w1g%d" % ph3)

    def emit_ffn_h(inject):
        w1_dma(0)
        w1_dma(1)
        for ich in range(32):
            if ich + 2 < 32:
                w1_dma(ich + 2)
            slot = ich % 3
            for s in range(NS):
                S0 = ST * s
                ip = (4 * ich + s) % 2
                for ex in range(4):
                    hb = (ich * 16 + s * 4 + ex) % 5
                    ph = [pP[0], pP[1], pVK[0], pVK[1], pVK[2]][hb]
                    r_ph = [r_pP[0], r_pP[1], r_pVK[0], r_pVK[1], r_pVK[2]][hb]
                    for c8 in range(8):
                        em("tensor", lambda e, c8=c8, ex=ex, slot=slot, S0=S0, ph=ph: e.matmul(
                            ph[:], ar(8192 + slot * 4096 + ex * 1024 + 128 * c8, 128),
                            xn_s(c8, S0, ST), start=(c8 == 0), stop=(c8 == 7)),
                           reads=[r_w1s[slot], r_xn], writes=[r_ph])
                    gp_, r_gp = ((g_sb[0], r_g[0]), (g_sb[1], r_g[1]),
                                 (g_x2, r_gx2), (g_x3, r_gx3))[ex]
                    em("scalar", lambda e, ex=ex, ich=ich, ph=ph, gp_=gp_: e.activation(
                        gp_, ph[:], AF.Gelu, bias=b1t_sb[:, 32 * ex + ich:32 * ex + ich + 1]),
                       reads=[r_ph, r_b1t], writes=[r_gp])
                    mi = 4 * s + ex
                    if ex == 0:
                        em("vector", lambda e, ip=ip, gp_=gp_, mi=mi: e.tensor_mul(
                            ar(AIP[ip], 512), gp_, m16_ap(mi)),
                           reads=[r_gp, r_m16[mi]], writes=[r_intp[ip]])
                    else:
                        mc, r_mc = (mac_t, r_mac) if ex % 2 else (mac2, r_mac2)
                        em("vector", lambda e, gp_=gp_, mc=mc, mi=mi: e.tensor_mul(
                            mc, gp_, m16_ap(mi)),
                           reads=[r_m16[mi], r_gp], writes=[r_mc])
                        em("vector", lambda e, ip=ip, mc=mc: e.tensor_add(
                            ar(AIP[ip], 512), ar(AIP[ip], 512), mc),
                           reads=[r_mc, r_intp[ip]], writes=[r_intp[ip]])
                em("sync", lambda e, ich=ich, s=s, ip=ip: e.dma_start(
                    out=AP(int_scr.tensor, ich * 262144 + 512 * s, [[2048, 128], [1, 512]]),
                    in_=ar(AIP[ip], 512)),
                   reads=[r_intp[ip]], writes=[r_iscr[ich][s]], dma="is%d" % ip)
            if inject:
                inject.pop(0)()
        while inject:
            inject.pop(0)()

    pY8 = [pP[0], pP[1], pP[2], pP[3], pVK[0], pVK[1], pVK[2], pAux]
    r_pY8 = [r_pP[0], r_pP[1], r_pP[2], r_pP[3], r_pVK[0], r_pVK[1], r_pVK[2], r_pAux]

    WOBQ = [0, 4096, 8192, 12288]
    AIPQ = [16384, 18432, 20480, 22528]
    r_wob4 = [Rs(f"wob4_{i}") for i in range(4)]
    r_yip4 = [Rs(f"yip4_{i}") for i in range(4)]

    def emit_ffn_y(s):
        if s == 0:
            alias(r_wob4[0], [r_wq])
            alias(r_wob4[1], [r_wq])
            alias(r_wob4[2], [r_w1s[0]])
            alias(r_wob4[3], [r_w1s[1]])
            alias(r_yip4[0], [r_w1s[2]])
            alias(r_yip4[1], [r_w1s[2]])
            alias(r_yip4[2], [r_wob[0], r_wob[1], r_x_tm[2]])
            alias(r_yip4[3], [r_intp[0], r_intp[1]])
        for q in range(8):
            sl = q % 4
            weng, wstream = (("scalar", "wob%d" % sl) if sl < 2
                             else ("gpsimd", "wobg%d" % (sl - 2)))
            em(weng, lambda e, q=q, sl=sl: e.dma_start(
                out=ar(WOBQ[sl], 4096),
                in_=AP(wo_bf_d.tensor, q * 524288, [[1024, 128], [131072, 4], [1, 1024]])),
               reads=[r_wobf[q // 2]], writes=[r_wob4[sl]], dma=wstream)
            em("sync", lambda e, q=q, s=s, sl=sl: e.dma_start(
                out=ar(AIPQ[sl], 2048),
                in_=AP(int_scr.tensor, q * 1048576 + 512 * s, [[2048, 128], [262144, 4], [1, 512]])),
               reads=[r_iscr[4 * q + j][s] for j in range(4)],
               writes=[r_yip4[sl]], dma="ir%d" % sl)
            for m in range(4):
                ich = 4 * q + m
                for b4 in range(4):
                    for hf in range(2):
                        em("tensor", lambda e, b4=b4, hf=hf, m=m, ich=ich, sl=sl: e.matmul(
                            pY8[2 * b4 + hf][:], ar(AIPQ[sl] + 512 * m + 128 * b4, 128),
                            ar(WOBQ[sl] + 1024 * m + 512 * hf, 512),
                            start=(ich == 0), stop=(ich == 31)),
                           reads=[r_yip4[sl], r_wob4[sl]], writes=[r_pY8[2 * b4 + hf]])
        for b4 in range(4):
            for hf in range(2):
                em("vector", lambda e, b4=b4, hf=hf: e.tensor_add(
                    y_sb[b4][:, 512 * hf:512 * hf + 512], pY8[2 * b4 + hf][:],
                    bob[:, 512 * hf:512 * hf + 512]),
                   reads=[r_pY8[2 * b4 + hf], r_bob], writes=[r_y[b4]])
        for b4 in range(4):
            blk = 4 * s + b4
            em("sync", lambda e, blk=blk, b4=b4: e.dma_start(
                out=yscr_d[128 * blk:128 * blk + 128, :], in_=y_sb[b4]),
               reads=[r_y[b4]], writes=[r_yscr[blk]], dma="ys")

    em("gpsimd", lambda e: e.dma_start(
        out=wr_f, in_=AP(wr_d.tensor, 0, [[4, 128], [512, 8], [1, 4]])),
       writes=[r_wr_f], dma="row")
    em("vector", lambda e: e.tensor_copy(wrh[:], wr_f), reads=[r_wr_f], writes=[r_wrh])
    em("vector", lambda e: e.tensor_sub(wrl[:], wr_f, wrh[:]),
       reads=[r_wr_f, r_wrh], writes=[r_wrl])
    em("gpsimd", lambda e: e.dma_start(
        out=b1t_sb[:], in_=AP(b1t_d.tensor, 0, [[32, 128], [4096, 4], [1, 32]])),
       writes=[r_b1t], dma="row")
    em("gpsimd", lambda e: e.dma_start(
        out=woutT_sb[:], in_=AP(woutT_d.tensor, 0, [[1024, 128], [131072, 8], [1, 1024]])),
       writes=[r_wout], dma="row")

    em("vector", lambda e: e.tensor_copy(rop1, iot1[:, 0:32]), reads=[r_iot1], writes=[r_rop1])
    em("scalar", lambda e: e.activation(invf2[:], rop1, AF.Exp, scale=-math.log(THETA) / 32.0),
       reads=[r_rop1], writes=[r_invf2])
    em("vector", lambda e: e.tensor_scalar_mul(invf2[:], invf2[:], 1.0 / (2 * math.pi)),
       reads=[r_invf2], writes=[r_invf2])
    def emit_ropes_all():
        ropP = scrF[:, 3072:3584]
        ropA = scrF[:, 1024:1536]
        ropBi = scrF[:, 1536:2048].bitcast(I32)[:, 0:512]
        ropC = scrF[:, 2048:2560]
        ropD = scrF[:, 2560:3072]
        invf_bc = AP(invf2[:].tensor, 0, [[32, 128], [0, 16], [1, 32]])
        em("gpsimd", lambda e: e.iota(ropP, pattern=[[128, 16], [0, 32]], base=0,
                                      channel_multiplier=1,
                                      allow_small_or_imprecise_dtypes=True),
           writes=[rF[3]])
        for shift in (0.25, 0.0):
            em("vector", lambda e: e.tensor_tensor(ropA, ropP, invf_bc, ALU.mult),
               reads=[rF[3], r_invf2], writes=[rF[1]])
            if shift:
                em("vector", lambda e, sh=shift: e.tensor_scalar_add(ropA, ropA, sh),
                   reads=[rF[1]], writes=[rF[1]])
            em("vector", lambda e: e.tensor_copy(ropBi, ropA), reads=[rF[1]], writes=[rF[1]])
            em("vector", lambda e: e.tensor_copy(ropC, ropBi), reads=[rF[1]], writes=[rF[2]])
            em("vector", lambda e: e.tensor_sub(ropC, ropA, ropC),
               reads=[rF[1], rF[2]], writes=[rF[2]])
            em("vector", lambda e: e.tensor_scalar_mul(ropC, ropC, 2 * math.pi),
               reads=[rF[2]], writes=[rF[2]])
            em("vector", lambda e: e.tensor_single_scalar(ropD, ropC, math.pi, ALU.is_gt),
               reads=[rF[2]], writes=[rF[2]])
            em("vector", lambda e: e.scalar_tensor_tensor(ropC, ropD, -2 * math.pi, ropC,
                                                          ALU.mult, ALU.add),
               reads=[rF[2]], writes=[rF[2]])
            em("vector", lambda e: e.tensor_scalar(ropC, ropC, math.pi, -math.pi,
                                                   ALU.min, ALU.max),
               reads=[rF[2]], writes=[rF[2]])
            if shift:
                em("scalar", lambda e: e.activation(cos_all[:, 0:512], ropC, AF.Sin),
                   reads=[rF[2]], writes=[r_cos])
            else:
                em("scalar", lambda e: e.activation(ropA, ropC, AF.Sin),
                   reads=[rF[2]], writes=[rF[1]])
        sup = AP(spair[:].tensor, 32, [[NB * 64, 128], [64, 16], [1, 32]])
        slo = AP(spair[:].tensor, 0, [[NB * 64, 128], [64, 16], [1, 32]])
        em("vector", lambda e, sup=sup: e.tensor_copy(sup, ropA),
           reads=[rF[1]], writes=[r_spair])
        em("vector", lambda e, slo=slo: e.tensor_scalar_mul(slo, ropA, -1.0),
           reads=[rF[1]], writes=[r_spair])

    def bcast_build(row_dram, ncols, dst, r_dst):
        em("gpsimd", lambda e: e.dma_start(out=rowbf[0:1, 0:ncols], in_=row_dram),
           writes=[r_rowbf], dma="row")
        for j0 in range(0, ncols, 512):
            w = min(512, ncols - j0)
            em("tensor", lambda e, j0=j0, w=w: e.matmul(
                pAux[:, 0:w], ones1[:], rowbf[0:1, j0:j0 + w], start=True, stop=True),
               reads=[r_ones1, r_rowbf], writes=[r_pAux])
            em("scalar", lambda e, j0=j0, w=w, dst=dst: e.activation(
                dst[:, j0:j0 + w], pAux[:, 0:w], AF.Copy),
               reads=[r_pAux], writes=[r_dst])

    def emit_bcasts():
        bcast_build(bqkv_d, 3 * D, bqb, r_bqb)
        bcast_build(gq_d, D, gqb, r_gqb)
        bcast_build(gk_d, D, gkb, r_gkb)
        bcast_build(bout_d, D, boutb, r_boutb)
        bcast_build(gout_d, D, goutb, r_goutb)
        bcast_build(bo_d, D, bob, r_bob)
        bcast_build(gamma_d, D, gammab, r_gammab)
        bcast_build(beta_d, D, betab, r_betab)
        for _rs in (r_y[1], r_y[2], r_y[3]):
            alias(_rs, [_rs, r_rowbf])

    r_wobf = [Rs(f"wobf{c}") for c in range(4)]
    for c in range(4):
        em("gpsimd", lambda e, c=c: e.dma_start(
            out=AP(wo_bf_d.tensor, c * 1048576, [[1024, 1024], [1, 1024]]),
            in_=AP(wo_d.tensor, c * 1048576, [[1024, 1024], [1, 1024]])),
           writes=[r_wobf[c]], dma=("wk", "wv", "wobg0", "wobg1")[c])
    for b0 in range(0, NB, 2):
        emit_a0x(b0)
        emit_a0x(b0 + 1)
        emit_a0a_sums(b0)
        emit_a0a_sums(b0 + 1)
        emit_a0a_fin(b0)
        for blk in (b0, b0 + 1):
            xb, r_xb = (xn_t, rF[3]) if blk % 2 == 0 else (t1_t, rF[0])
            emit_a0ap(blk, xb, r_xb)
            emit_a0b(blk, xb, r_xb)
            emit_a0c_mm(blk, blk - b0)
        emit_a0c_cas(b0)
        if b0 == 12:
            emit_bcasts()
    emit_ropes_all()
    emit_masks()
    alias(r_mac2, [r_mac2, r_xlB])
    alias(r_gx2, [r_gx2, r_xlB])
    inj = []
    for blk in range(NB):
        inj.append(lambda b=blk: emit_a1a(b))
        inj.append(lambda b=blk: emit_a1b(b))
    emit_ffn_h(inj)
    alias(r_y[0], [r_y[0], r_m16[11], r_m16[12]])
    alias(r_y[1], [r_y[1], r_m16[13], r_m16[14]])
    alias(r_y[2], [r_y[2], r_m16[15]])
    for s in range(NS):
        emit_ffn_y(s)

    alias(r_wk, [r_w1s[0], r_w1s[1], r_wob4[2], r_wob4[3]])
    alias(r_kf[0], [r_kf[0], r_mac2, r_gx2])
    alias(r_kf[1], [r_kf[1], r_m16[2], r_m16[3]])
    alias(r_v[0], [r_v[0], r_m16[4], r_m16[5]])
    alias(r_wv, [r_w1s[2], r_wob[0], r_wob[1], r_intp[0], r_intp[1], r_x_tm[2],
                 r_yip4[0], r_yip4[1], r_yip4[2], r_yip4[3]])
    em("gpsimd", lambda e: e.dma_start(
        out=ar(AK, 8192), in_=AP(wqkvT_d.tensor, 1024, [[3072, 128], [393216, 8], [1, 1024]])),
       writes=[r_wk], dma="wk")
    em("gpsimd", lambda e: e.dma_start(
        out=ar(AV, 8192), in_=AP(wqkvT_d.tensor, 2048, [[3072, 128], [393216, 8], [1, 1024]])),
       writes=[r_wv], dma="wv")
    alias(r_v[1], [r_maskb, r_m16[6], r_m16[7]])
    a2_kmms(0, pP[0], pP[1], r_pP[0], r_pP[1])
    a2_kmms(1, pP[2], pP[3], r_pP[2], r_pP[3])
    piece_sums(pP[0], pP[1], r_pP[0], r_pP[1], qb16, r_q_t, 0)
    piece_sums(pP[2], pP[3], r_pP[2], r_pP[3], xb316, rF[3], 1)
    piece_batfin()
    for b0 in range(0, NB, 2):
        piece_fin(b0, qb16, r_q_t, kf_t[0], r_kf[0], 0)
        piece_fin(b0 + 1, xb316, rF[3], kf_t[1], r_kf[1], 1)
        a2_vmms_evac(b0, pP[0], pP[1], r_pP[0], r_pP[1], 0)
        a2_vmms_evac(b0 + 1, pP[2], pP[3], r_pP[2], r_pP[3], 1)
        if b0 + 2 < NB:
            a2_kmms(b0 + 2, pP[0], pP[1], r_pP[0], r_pP[1])
            a2_kmms(b0 + 3, pP[2], pP[3], r_pP[2], r_pP[3])
            piece_sums(pP[0], pP[1], r_pP[0], r_pP[1], qb16, r_q_t, 0)
            piece_sums(pP[2], pP[3], r_pP[2], r_pP[3], xb316, rF[3], 1)
            piece_batfin()
        emit_a2_vk(b0)
        emit_a2_vk(b0 + 1)

    for p in range(8):
        bank = p // 3 if p < 6 else 2
        off = (p % 3) * 129 if p < 6 else (p - 6) * 129
        pv, r_pv = pVK[bank], r_pVK[bank]
        em("vector", lambda e, p=p, off=off, pv=pv: e.tensor_copy(
            vkN[0:64, 128 * p:128 * p + 64], pv[0:64, off:off + 64]),
           reads=[r_pv], writes=[r_vkN])
        em("vector", lambda e, p=p, off=off, pv=pv: e.tensor_copy(
            vkN[64:128, 128 * p + 64:128 * p + 128], pv[64:128, off + 65:off + 129]),
           reads=[r_pv], writes=[r_vkN])
        em("vector", lambda e, p=p, off=off, pv=pv: e.tensor_copy(
            sfull[0:64, 18 * p:18 * p + 1], pv[0:64, off + 64:off + 65]),
           reads=[r_pv], writes=[r_sfull])
        em("vector", lambda e, p=p, off=off, pv=pv: e.tensor_copy(
            sfull[64:128, 18 * p + 1:18 * p + 2], pv[64:128, off + 64:off + 65]),
           reads=[r_pv], writes=[r_sfull])

    r_attn8 = [Rs(f"attn8_{c}") for c in range(8)]
    for c in range(8):
        alias(r_attn8[c], [r_attn, r_xl, r_kf[0], r_kf[1], r_v[0], r_v[1], r_rowbf]
              + r_m16[0:10])

    arenaF = arena[:, 0:24576].bitcast(F32)
    xnF = xn_fm[:, 0:16384].bitcast(F32)
    zb_t = [arena[:, 1024 * i:1024 * i + 1024] for i in range(4)]
    xnb_t = [arena[:, 4096 + 1024 * i:4096 + 1024 * i + 1024] for i in range(4)]
    junkF4 = [arenaF[:, 4096 + 1024 * i:4096 + 1024 * i + 1024] for i in range(4)]
    yb_t = [arena[:, 16384 + 1024 * i:16384 + 1024 * i + 1024] for i in range(4)]
    x4 = [[xnF[:, 4096 * h + 1024 * i:4096 * h + 1024 * i + 1024] for i in range(4)]
          for h in range(2)]
    r_zb = [Rs(f"tzb{i}") for i in range(4)]
    r_xnb = [Rs(f"txnb{i}") for i in range(4)]
    r_junkq = [Rs(f"tjunk{i}") for i in range(4)]
    r_yb = [Rs(f"tyb{i}") for i in range(4)]
    r_x4 = [[Rs(f"tx4_{h}{i}") for i in range(4)] for h in range(2)]
    r_stq = [Rs(f"stq{i}") for i in range(4)]
    r_sty = [Rs(f"sty{i}") for i in range(4)]
    for i in range(4):
        alias(r_zb[i], [r_wq, r_wob4[0], r_wob4[1]])
        alias(r_xnb[i], [r_wq, r_wob4[0], r_wob4[1]])
        alias(r_junkq[i], [r_wk, r_wob4[2], r_wob4[3]])
        alias(r_yb[i], [r_wv, r_yip4[0], r_yip4[1]])
        alias(r_x4[0][i], [r_xn])
        alias(r_x4[1][i], [r_xn])

    alias(r_rd, [r_rd, r_maskb] + r_m16[6:10])
    alias(r_den, [r_den, r_gx3])
    for s in range(NS):
        S0 = ST * s
        for c8 in range(8):
            em("tensor", lambda e, c8=c8, S0=S0: e.matmul(
                pAux[0:16, :], sfull[:, 16 * c8:16 * c8 + 16], qf_s(c8, S0, ST),
                start=(c8 == 0), stop=(c8 == 7)),
               reads=[r_sfull, r_qf], writes=[r_pAux])
        em("vector", lambda e, s=s: e.tensor_scalar_add(
            rd_bf[:, 512 * s:512 * s + 512], pAux[0:16, :], EPS_LIN),
           reads=[r_pAux], writes=[r_rd])
    em("vector", lambda e: e.reciprocal(rd_bf, rd_bf), reads=[r_rd], writes=[r_rd])
    def emit_tail_front(s):
        S0 = ST * s
        h = s % 2
        for b4 in range(4):
            blk = 4 * s + b4
            em("sync", lambda e, blk=blk, h=h, b4=b4: e.dma_start(
                out=x4[h][b4], in_=x_d[128 * blk:128 * blk + 128, :]),
               writes=[r_x4[h][b4]], dma="x%d" % (b4 % 2))
        for c8 in range(8):
            prd, r_prd = pP[c8 % 2], r_pP[c8 % 2]
            pat, r_pat = pP[2 + c8 % 2], r_pP[2 + c8 % 2]
            em("tensor", lambda e, c8=c8, S0=S0, prd=prd: e.matmul(
                prd[:], selheads[:, 128 * c8:128 * c8 + 128], rd_bf[:, S0:S0 + ST],
                start=True, stop=True),
               reads=[r_selh, r_rd], writes=[r_prd])
            qp, r_qp = (qfp_sb, r_qfp) if c8 % 2 == 0 else (g_sb[0], r_g[0])
            em("vector", lambda e, c8=c8, S0=S0, prd=prd, qp=qp: e.tensor_mul(
                qp, prd[:], qf_s(c8, S0, ST)),
               reads=[r_prd, r_qf], writes=[r_qp])
            em("tensor", lambda e, c8=c8, pat=pat, qp=qp: e.matmul(
                pat[:], vkN[:, 128 * c8:128 * c8 + 128], qp, start=True, stop=True),
               reads=[r_vkN, r_qp], writes=[r_pat])
            em("scalar", lambda e, c8=c8, pat=pat: e.activation(
                attn_sb[:, 512 * c8:512 * c8 + 512], pat[:], AF.Copy),
               reads=[r_pat], writes=[r_attn8[c8]])

    ZBANKS = [(pVK[0], pVK[1]), (pVK[2], pAux), (pP[0], pP[1]), (pP[2], pP[3])]
    ZRBANKS = [(r_pVK[0], r_pVK[1]), (r_pVK[2], r_pAux), (r_pP[0], r_pP[1]),
               (r_pP[2], r_pP[3])]

    def emit_tail_z_mms(s):
        for b4 in range(4):
            pz = list(ZBANKS[b4])
            r_pz = list(ZRBANKS[b4])
            for c8 in range(8):
                for half in range(2):
                    em("tensor", lambda e, c8=c8, b4=b4, half=half, pz=pz: e.matmul(
                        pz[half][:], attn_sb[:, 512 * c8 + 128 * b4:512 * c8 + 128 * b4 + 128],
                        woutT_sb[:, 1024 * c8 + 512 * half:1024 * c8 + 512 * half + 512],
                        start=(c8 == 0), stop=(c8 == 7)),
                       reads=[r_attn8[c8], r_wout], writes=[r_pz[half]])

    def emit_tail_z_evac(s):
        for b4 in range(4):
            pz = list(ZBANKS[b4])
            r_pz = list(ZRBANKS[b4])
            for half in range(2):
                em("vector", lambda e, half=half, b4=b4, pz=pz: e.tensor_add(
                    zb_t[b4][:, 512 * half:512 * half + 512], pz[half][:],
                    boutb[:, 512 * half:512 * half + 512]),
                   reads=[r_pz[half], r_boutb], writes=[r_zb[b4]])
            em("scalar", lambda e, b4=b4: e.activation(junkF4[b4], zb_t[b4], AF.Square,
                                                       accum_out=stc[:, 16 + b4:17 + b4]),
               reads=[r_zb[b4]], writes=[r_junkq[b4], r_stq[b4]])

    def emit_tail_rest(s):
        S0 = ST * s
        h = s % 2
        for b4 in range(4):
            blk = 4 * s + b4
            em("sync", lambda e, blk=blk, b4=b4: e.dma_start(
                out=y_sb[b4], in_=yscr_d[128 * blk:128 * blk + 128, :]),
               reads=[r_yscr[blk]], writes=[r_y[b4]], dma="yr%d" % (b4 % 2))
        em("vector", lambda e: e.tensor_scalar(stc[:, 4:8], stc[:, 16:20], 1.0 / D, RMS_EPS,
                                               ALU.mult, ALU.add),
           reads=r_stq, writes=[r_stc])
        rsqrt_batch()
        for b4 in range(4):
            em("scalar", lambda e, b4=b4: e.activation(zb_t[b4], zb_t[b4], AF.Identity,
                                                       scale=stc[:, 24 + b4:25 + b4]),
               reads=[r_stc, r_zb[b4]], writes=[r_zb[b4]])
            em("vector", lambda e, b4=b4: e.tensor_mul(zb_t[b4], zb_t[b4], goutb[:]),
               reads=[r_goutb, r_zb[b4]], writes=[r_zb[b4]])
        for b4 in range(4):
            blk = 4 * s + b4
            em("vector", lambda e, blk=blk, h=h, b4=b4: e.tensor_scalar(
                xnb_t[b4], x4[h][b4], lnmu[:, blk:blk + 1], lnis[:, blk:blk + 1],
                ALU.subtract, ALU.mult),
               reads=[r_x4[h][b4], r_lnmu, r_lnis], writes=[r_xnb[b4]])
            em("vector", lambda e, b4=b4: e.scalar_tensor_tensor(
                yb_t[b4], y_sb[b4], 1.0, xnb_t[b4], ALU.mult, ALU.add,
                accum_out=stc[:, 28 + b4:29 + b4]),
               reads=[r_y[b4], r_xnb[b4]], writes=[r_yb[b4], r_sty[b4]])
            em("scalar", lambda e, b4=b4: e.activation(junkF4[b4], yb_t[b4], AF.Square,
                                                       accum_out=stc[:, 16 + b4:17 + b4]),
               reads=[r_yb[b4]], writes=[r_junkq[b4], r_stq[b4]])
        em("vector", lambda e: e.tensor_scalar_mul(stc[:, 32:36], stc[:, 28:32], 1.0 / D),
           reads=r_sty, writes=[r_stc])
        em("vector", lambda e: e.tensor_scalar(stc[:, 4:8], stc[:, 16:20], 1.0 / D, FFN_LN_EPS,
                                               ALU.mult, ALU.add),
           reads=r_stq, writes=[r_stc])
        em("vector", lambda e: e.tensor_mul(stc[:, 20:24], stc[:, 32:36], stc[:, 32:36]),
           reads=[r_stc], writes=[r_stc])
        em("vector", lambda e: e.tensor_sub(stc[:, 4:8], stc[:, 4:8], stc[:, 20:24]),
           reads=[r_stc], writes=[r_stc])
        rsqrt_batch()
        em("vector", lambda e: e.scalar_tensor_tensor(
            stc[:, 36:40], stc[:, 32:36], -1.0, stc[:, 24:28], ALU.mult, ALU.mult),
           reads=[r_stc], writes=[r_stc])
        for b4 in range(4):
            em("scalar", lambda e, b4=b4: e.activation(
                yb_t[b4], yb_t[b4], AF.Identity,
                scale=stc[:, 24 + b4:25 + b4], bias=stc[:, 36 + b4:37 + b4]),
               reads=[r_stc, r_yb[b4]], writes=[r_yb[b4]])
            em("vector", lambda e, b4=b4: e.tensor_mul(yb_t[b4], yb_t[b4], gammab[:]),
               reads=[r_gammab, r_yb[b4]], writes=[r_yb[b4]])
            em("vector", lambda e, b4=b4: e.tensor_add(yb_t[b4], yb_t[b4], betab[:]),
               reads=[r_betab, r_yb[b4]], writes=[r_yb[b4]])
        for b4 in range(4):
            em("vector", lambda e, b4=b4, h=h: e.tensor_add(f_(b4), x4[h][b4], zb_t[b4]),
               reads=[r_x4[h][b4], r_zb[b4]], writes=[rF[b4]])
        for b4 in range(4):
            blk = 4 * s + b4
            em("vector", lambda e, b4=b4: e.tensor_add(f_(b4), f_(b4), yb_t[b4]),
               reads=[r_yb[b4], rF[b4]], writes=[rF[b4]])
            em("sync", lambda e, blk=blk, b4=b4: e.dma_start(
                out=out_d[128 * blk:128 * blk + 128, :], in_=f_(b4)),
               reads=[rF[b4]], dma="o%d" % (b4 % 2))

    emit_tail_front(0)
    emit_tail_z_mms(0)
    emit_tail_z_evac(0)
    for s in range(NS):
        if s + 1 < NS:
            emit_tail_front(s + 1)
            emit_tail_z_mms(s + 1)
        emit_tail_rest(s)
        if s + 1 < NS:
            emit_tail_z_evac(s + 1)
    for stream in ("o0", "o1"):
        key = "syncd_" + stream
        if key in b.cnt:
            b.prog["sync"].append(("wait", key, b.cnt[key]))

    sems = {}
    for name in sorted(b.cnt):
        sems[name] = EN(nc.semaphore(f"s_{name}"))
    blk_ctx = EN(nc.Block())

    def replay(eng_name):
        def run(eng):
            for item in b.prog[eng_name]:
                if item[0] == "wait":
                    eng.wait_ge(sems[item[1]], item[2])
                elif item[0] == "drain":
                    eng.drain()
                else:
                    _, fn, sem, inc = item
                    fn(eng).then_inc(sems[sem], inc)
        return run

    with nc.allow_low_precision(reason="bf16 matmul operand staging; accumulation is fp32 PSUM"):
        blk_ctx.sync(replay("sync"))
        blk_ctx.scalar(replay("scalar"))
        blk_ctx.vector(replay("vector"))
        blk_ctx.tensor(replay("tensor"))
        blk_ctx.gpsimd(replay("gpsimd"))
    ctx.close()
    return nc


_PERM64 = np.concatenate([np.arange(0, 64, 2), np.arange(1, 64, 2)])


def _perm_d():
    return np.concatenate([64 * h + _PERM64 for h in range(NH)])


def kernel(L0_attn_x, L0_attn_Wqkv, L0_attn_bqkv, L0_attn_gq, L0_attn_gk, L0_attn_Wout,
           L0_attn_bout, L0_attn_gout, L0_ffn_W_router, L0_ffn_W1, L0_ffn_b1, L0_ffn_Wo,
           L0_ffn_bo, L0_ffn_ln_gamma, L0_ffn_ln_beta):
    x = np.ascontiguousarray(np.asarray(L0_attn_x, dtype=np.float32))
    perm = _perm_d()
    wqkv = np.asarray(L0_attn_Wqkv, dtype=np.float32)
    wqkv_p = wqkv.copy()
    wqkv_p[0:D] = wqkv[0:D][perm]
    wqkv_p[D:2 * D] = wqkv[D:2 * D][perm]
    wqkvT = np.ascontiguousarray(wqkv_p.T)
    bqkv = np.asarray(L0_attn_bqkv, dtype=np.float32).copy()
    bqkv[0:D] = bqkv[0:D][perm]
    bqkv[D:2 * D] = bqkv[D:2 * D][perm]
    bqkv = bqkv.reshape(1, 3 * D)
    gq = np.ascontiguousarray(np.asarray(L0_attn_gq, dtype=np.float32)[perm]).reshape(1, D)
    gk = np.ascontiguousarray(np.asarray(L0_attn_gk, dtype=np.float32)[perm]).reshape(1, D)
    woutT = np.ascontiguousarray(np.asarray(L0_attn_Wout, dtype=np.float32).T)
    bout = np.asarray(L0_attn_bout, dtype=np.float32).reshape(1, D)
    gout = np.asarray(L0_attn_gout, dtype=np.float32).reshape(1, D)
    wr = np.ascontiguousarray(np.asarray(L0_ffn_W_router, dtype=np.float32))
    w1 = np.asarray(L0_ffn_W1, dtype=np.float32)
    w1 = np.ascontiguousarray(
        w1.reshape(E, 8, 128, 32, 128).transpose(0, 3, 2, 1, 4).reshape(E * 32, 128, 1024))
    b1t = np.ascontiguousarray(
        np.asarray(L0_ffn_b1, dtype=np.float32).reshape(E, 32, 128).transpose(0, 2, 1))
    wo = np.ascontiguousarray(np.asarray(L0_ffn_Wo, dtype=np.float32))
    bo = np.asarray(L0_ffn_bo, dtype=np.float32).reshape(1, D)
    gamma = np.asarray(L0_ffn_ln_gamma, dtype=np.float32).reshape(1, D)
    beta = np.asarray(L0_ffn_ln_beta, dtype=np.float32).reshape(1, D)

    nc = build_nc()
    shared = dict(wqkvT=wqkvT, bqkv=bqkv, gq=gq, gk=gk, wr=wr, woutT=woutT, bout=bout,
                  gout=gout, w1=w1, b1t=b1t, wo=wo, bo=bo, gamma=gamma, beta=beta)
    in_maps = [dict(x=np.ascontiguousarray(x[c]), **shared) for c in range(8)]
    res = run_bass_kernel_spmd(nc, in_maps, core_ids=list(range(8)))
    out = np.stack([res.results[c]["out"] for c in range(8)], axis=0)
    return out.astype(np.float32)


# version 38
# speedup vs baseline: 1.0095x; 1.0095x over previous
import math
import numpy as np

import concourse.bass as bass
import concourse.mybir as mybir
from concourse.ap import AP
from concourse.bass_utils import run_bass_kernel_spmd

F32 = mybir.dt.float32
U16 = mybir.dt.uint16
BF16 = mybir.dt.bfloat16
I32 = mybir.dt.int32
AF = mybir.ActivationFunctionType
ALU = mybir.AluOpType
AX = mybir.AxisListType

T = 2048
D = 1024
NB = 16
NH = 16
DH = 64
E = 4
DI = 4096
NS = 4
ST = 512
LN_EPS = 1e-5
RMS_EPS = 1.1920929e-07
FFN_LN_EPS = 1e-12
EPS_LIN = 1e-6
THETA = 10000.0

ENG = ("sync", "scalar", "vector", "tensor", "gpsimd")


class Rs:
    __slots__ = ("name", "w", "r", "psum")

    def __init__(self, name, psum=False):
        self.name = name
        self.w = None
        self.r = []
        self.psum = psum


class Bld:
    def __init__(self):
        self.prog = {e: [] for e in ENG}
        self.cnt = {}
        self.waited = {e: {} for e in ENG}
        self.drained = {e: 0 for e in ENG}
        for e in ENG:
            self.cnt[e + "c"] = 0
            self.cnt[e + "d"] = 0

    @staticmethod
    def tok_eng(tok):
        s = tok[0]
        return s.split("d_")[0] if "d_" in s else s[:-1]

    def emit(self, eng, fn, reads=(), writes=(), dma=None, no_selfwait=False):
        sem = (eng + "d_" + dma) if dma else (eng + "c")
        if dma and sem not in self.cnt:
            self.cnt[sem] = 0
        deps = {}

        def add(tok):
            if tok is None:
                return
            s, v = tok
            if deps.get(s, 0) < v:
                deps[s] = v

        for rs in reads:
            add(rs.w)
            if rs.psum:
                for t in rs.r:
                    if self.tok_eng(t) != eng:
                        add(t)
        for rs in writes:
            add(rs.w)
            for t in rs.r:
                add(t)

        own = eng + "c"
        need_drain = False
        for s, v in sorted(deps.items()):
            if s == own:
                if eng != "tensor" and v > self.drained[eng]:
                    need_drain = True
            else:
                if self.waited[eng].get(s, 0) < v:
                    self.prog[eng].append(("wait", s, v))
                    self.waited[eng][s] = v
        if need_drain:
            self.prog[eng].append(("drain",))
            self.drained[eng] = self.cnt[own]

        if dma and not no_selfwait and self.cnt[sem] > 0 and self.waited[eng].get(sem, 0) < self.cnt[sem]:
            self.prog[eng].append(("wait", sem, self.cnt[sem]))
            self.waited[eng][sem] = self.cnt[sem]

        inc = 16 if dma else 1
        self.cnt[sem] += inc
        tok = (sem, self.cnt[sem])
        self.prog[eng].append(("op", fn, sem, inc))
        for rs in reads:
            rs.r.append(tok)
        for rs in writes:
            rs.w = tok
            rs.r = []
        return tok


def build_nc():
    nc = bass.Bass("TRN2", target_bir_lowering=False, debug=False)
    b = Bld()
    em = b.emit

    x_d = nc.dram_tensor("x", [T, D], F32, kind="ExternalInput").ap()
    wqkvT_d = nc.dram_tensor("wqkvT", [D, 3 * D], F32, kind="ExternalInput").ap()
    bqkv_d = nc.dram_tensor("bqkv", [1, 3 * D], F32, kind="ExternalInput").ap()
    gq_d = nc.dram_tensor("gq", [1, D], F32, kind="ExternalInput").ap()
    gk_d = nc.dram_tensor("gk", [1, D], F32, kind="ExternalInput").ap()
    wr_d = nc.dram_tensor("wr", [D, E], F32, kind="ExternalInput").ap()
    woutT_d = nc.dram_tensor("woutT", [D, D], F32, kind="ExternalInput").ap()
    bout_d = nc.dram_tensor("bout", [1, D], F32, kind="ExternalInput").ap()
    gout_d = nc.dram_tensor("gout", [1, D], F32, kind="ExternalInput").ap()
    w1_d = nc.dram_tensor("w1", [E * 32, 128, 1024], F32, kind="ExternalInput").ap()
    b1t_d = nc.dram_tensor("b1t", [E, 128, 32], F32, kind="ExternalInput").ap()
    wo_d = nc.dram_tensor("wo", [DI, D], F32, kind="ExternalInput").ap()
    bo_d = nc.dram_tensor("bo", [1, D], F32, kind="ExternalInput").ap()
    gamma_d = nc.dram_tensor("gamma", [1, D], F32, kind="ExternalInput").ap()
    beta_d = nc.dram_tensor("beta", [1, D], F32, kind="ExternalInput").ap()
    out_d = nc.dram_tensor("out", [T, D], F32, kind="ExternalOutput").ap()
    yscr_d = nc.dram_tensor("y_scr", [T, D], BF16, kind="Internal").ap()
    wo_bf_d = nc.dram_tensor("wo_bf", [DI, D], BF16, kind="Internal").ap()

    import contextlib
    ctx = contextlib.ExitStack()
    EN = ctx.enter_context

    def sb(name, shape, dtype):
        t = EN(nc.sbuf_tensor(name, shape, dtype))
        return t, Rs(name)

    def ps(name):
        t = EN(nc.psum_tensor(name, [128, 512], F32))
        return t, Rs(name, psum=True)

    ident, r_ident = sb("ident", [128, 128], F32)
    identb, r_identb = sb("identb", [128, 128], BF16)
    ones1, r_ones1 = sb("ones1", [1, 128], BF16)
    selheads, r_selh = sb("selheads", [16, 1024], BF16)
    erows, r_erows = sb("erows", [4, 512], BF16)
    onescol, r_onescol = sb("onescol", [128, 1], BF16)
    cos_all, r_cos = sb("cos_all", [128, NB * 32], BF16)
    spair, r_spair = sb("spair", [128, NB * 64], BF16)
    gqb, r_gqb = sb("gqb", [128, D], BF16)
    gkb, r_gkb = sb("gkb", [128, D], BF16)
    bqb, r_bqb = sb("bqb", [128, 3 * D], BF16)
    boutb, r_boutb = sb("boutb", [128, D], BF16)
    bob, r_bob = sb("bob", [128, D], BF16)
    gammab, r_gammab = sb("gammab", [128, D], BF16)
    betab, r_betab = sb("betab", [128, D], BF16)
    goutb, r_goutb = sb("goutb", [128, D], BF16)
    b1t_sb, r_b1t = sb("b1t_sb", [128, 128], F32)
    wrh, r_wrh = sb("wrh", [128, 32], BF16)
    wrl, r_wrl = sb("wrl", [128, 32], BF16)
    lnmu, r_lnmu = sb("lnmu", [128, NB], F32)
    lnis, r_lnis = sb("lnis", [128, NB], F32)
    mask_tm, r_mask_tm = sb("mask_tm", [128, 4 * NB], F32)
    xn_fm, r_xn = sb("xn_fm", [128, 8 * T], BF16)
    qf_fm, r_qf = sb("qf_fm", [128, 8 * T], BF16)
    vkN, r_vkN = sb("vkN", [128, 1024], BF16)
    sfull, r_sfull = sb("sfull", [128, 128], BF16)
    woutT_sb, r_wout = sb("woutT_sb", [128, 8 * D], BF16)

    AR_COLS = 24576
    arena, _ = sb("arena", [128, AR_COLS], BF16)
    r_wq, r_wk, r_wv = Rs("wq"), Rs("wk"), Rs("wv")
    AQ, AK, AV = 0, 8192, 16384
    AW1 = [0, 2048]
    AWO = [4096, 5120]
    AI = 6144
    r_w1b = [Rs("w1b0"), Rs("w1b1")]
    r_wob = [Rs("wob0"), Rs("wob1")]
    r_int = [Rs(f"int{i}") for i in range(32)]

    def ar(c0, n):
        return arena[:, c0:c0 + n]

    scrF, _ = sb("scrF", [128, 4096], F32)
    scrB, _ = sb("scrB", [128, 12288], BF16)
    rF = [Rs(f"F{i}") for i in range(4)]

    def f_(i):
        return scrF[:, 1024 * i:1024 * i + 1024]

    xn_t = scrF[:, 3072:4096]
    t1_t = f_(0)
    q_t = f_(1)
    t2_t = f_(2)
    iof1 = scrF[:, 3072:3200]
    rop1 = scrF[:, 3200:3232]
    rop3 = scrF[:, 3232:3264]
    wr_f = scrF[:, 3264:3296]
    ropg = scrF[:, 3296:3328]
    r_xn_t = rF[3]
    r_t1 = rF[0]
    r_q_t = rF[1]
    r_t2 = rF[2]
    r_iof1 = r_rop1 = r_rop3 = r_wr_f = r_ropg = rF[3]

    xl_t = scrB[:, 0:1024]
    kf_t = [scrB[:, 1024:2048], scrB[:, 2048:3072]]
    v_t = [scrB[:, 3072:4096], scrB[:, 4096:5120]]
    rowbf = scrB[0:1, 9216:9216 + 3 * D]
    attn_sb = scrB[:, 0:4096]
    maskb = scrB[:, 4096:6144]
    qfp_sb = scrB[:, 6144:6656]
    g_sb = [scrB[:, 6656:7168], scrB[:, 6656:7168]]
    mac_t = scrB[:, 6144:6656]
    den_f = scrB[0:16, 7168:7680]
    rd_bf = scrB[0:16, 4096:6144]
    mrows = scrB[0:4, 7168:7680]
    y_sb = [scrB[:, 8192 + 1024 * i:8192 + 1024 * (i + 1)] for i in range(4)]
    r_xl = Rs("xl")
    r_xlB = Rs("xlB")
    r_kf = [Rs("kf0"), Rs("kf1")]
    r_v = [Rs("v0"), Rs("v1")]
    r_rowbf = Rs("rowbf")
    r_attn = Rs("attn")
    r_maskb = Rs("maskb")
    r_qfp = Rs("qfp")
    _rg = Rs("g")
    r_g = [_rg, _rg]
    r_mac = r_qfp
    mac2 = scrB[:, 1024:1536]
    r_mac2 = Rs("mac2")
    g_x2 = scrB[:, 1536:2048]
    r_gx2 = Rs("gx2")
    g_x3 = scrB[:, 7168:7680]
    r_gx3 = Rs("gx3")
    r_den = Rs("den")
    r_rd = Rs("rd")
    r_mrows = r_den
    r_y = [Rs(f"y{i}") for i in range(4)]

    stc, r_stc = sb("stc", [128, 40], F32)
    lg_f, r_lg_f = sb("lg_f", [4, 128], F32)
    lg_tm, r_lg_tm = sb("lg_tm", [128, 8], F32)
    eq_t, r_eq_t = sb("eq_t", [128, 4], F32)
    alr_t, r_alr_t = sb("alr_t", [128, 4], F32)
    iot1, r_iot1 = sb("iot1", [128, 128], I32)
    iot2, r_iot2 = sb("iot2", [128, 1], I32)
    pcol, r_pcol = sb("pcol", [128, 1], F32)
    invf2, r_invf2 = sb("invf2", [128, 32], F32)
    rop2, r_rop2 = sb("rop2", [128, 32], I32)
    x_tm = [None, None]
    r_x_tm = [None, None]
    x_tm[0], r_x_tm[0] = sb("x_tm0", [128, D], F32)
    x_tm[1], r_x_tm[1] = sb("x_tm1", [128, D], F32)
    x_tm.append(arena[:, 20480:22528].bitcast(F32)[:, 0:1024])
    r_x_tm.append(Rs("x_tm2"))

    pP, r_pP = [], []
    for i in range(4):
        t_, rr_ = ps(f"pP{i}")
        pP.append(t_)
        r_pP.append(rr_)
    pAux, r_pAux = ps("pAux")
    pVK, r_pVK = [], []
    for i in range(3):
        t_, rr_ = ps(f"pVK{i}")
        pVK.append(t_)
        r_pVK.append(rr_)

    def xn_s(c8, col0, n):
        return xn_fm[:, 2048 * c8 + col0: 2048 * c8 + col0 + n]

    def qf_s(c8, col0, n):
        return qf_fm[:, 2048 * c8 + col0: 2048 * c8 + col0 + n]

    def alias(new_rs, old_list):
        toks = []
        for o in old_list:
            if o.w:
                toks.append(o.w)
            toks.extend(o.r)
        new_rs.r = list(toks)
        new_rs.w = None

    em("gpsimd", lambda e: e.iota(iot1[:], pattern=[[1, 128]], base=0, channel_multiplier=0),
       writes=[r_iot1])
    em("gpsimd", lambda e: e.iota(iot2[:], pattern=[[0, 1]], base=0, channel_multiplier=1),
       writes=[r_iot2])
    em("vector", lambda e: e.tensor_copy(iof1, iot1[:]), reads=[r_iot1], writes=[r_iof1])
    em("vector", lambda e: e.tensor_copy(pcol[:], iot2[:]), reads=[r_iot2], writes=[r_pcol])
    em("vector", lambda e: e.tensor_scalar(ident[:], iof1, pcol[:], None, ALU.is_equal),
       reads=[r_iof1, r_pcol], writes=[r_ident])
    em("vector", lambda e: e.tensor_copy(identb[:], ident[:]), reads=[r_ident], writes=[r_identb])
    em("vector", lambda e: e.memset(ones1[:], 1.0), writes=[r_ones1])
    em("vector", lambda e: e.memset(onescol[:], 1.0), writes=[r_onescol])
    em("gpsimd", lambda e: e.iota(selheads[:], pattern=[[-1, 16], [0, 64]], base=0,
                                  channel_multiplier=1, allow_small_or_imprecise_dtypes=True),
       writes=[r_selh])
    em("vector", lambda e: e.tensor_single_scalar(selheads[:], selheads[:], 0.0, ALU.is_equal),
       reads=[r_selh], writes=[r_selh])
    em("gpsimd", lambda e: e.iota(erows[:], pattern=[[-1, 4], [0, 128]], base=0,
                                  channel_multiplier=1, allow_small_or_imprecise_dtypes=True),
       writes=[r_erows])
    em("vector", lambda e: e.tensor_single_scalar(erows[:], erows[:], 0.0, ALU.is_equal),
       reads=[r_erows], writes=[r_erows])
    em("vector", lambda e: e.memset(vkN[:], 0.0), writes=[r_vkN])
    em("vector", lambda e: e.memset(sfull[:], 0.0), writes=[r_sfull])

    em("gpsimd", lambda e: e.dma_start(
        out=ar(AQ, 8192), in_=AP(wqkvT_d.tensor, 0, [[3072, 128], [393216, 8], [1, 1024]])),
       writes=[r_wq], dma="wq")


    def rsqrt_steps():
        ti = stc[:, 4:5].bitcast(I32)
        yi = stc[:, 9:10].bitcast(I32)
        em("vector", lambda e: e.tensor_single_scalar(yi, ti, 1, ALU.arith_shift_right),
           reads=[r_stc], writes=[r_stc])
        em("vector", lambda e: e.tensor_scalar(yi, yi, -1, 0x5f3759df, ALU.mult, ALU.add),
           reads=[r_stc], writes=[r_stc])
        for _ in range(3):
            em("vector", lambda e: e.tensor_mul(stc[:, 7:8], stc[:, 9:10], stc[:, 9:10]),
               reads=[r_stc], writes=[r_stc])
            em("vector", lambda e: e.tensor_scalar(stc[:, 7:8], stc[:, 7:8], stc[:, 4:5], -0.5,
                                                   ALU.mult, ALU.mult),
               reads=[r_stc], writes=[r_stc])
            em("vector", lambda e: e.tensor_scalar_add(stc[:, 7:8], stc[:, 7:8], 1.5),
               reads=[r_stc], writes=[r_stc])
            em("vector", lambda e: e.tensor_mul(stc[:, 9:10], stc[:, 9:10], stc[:, 7:8]),
               reads=[r_stc], writes=[r_stc])

    def rsqrt_batch(wd=4):
        ti = stc[:, 4:4 + wd].bitcast(I32)
        yi = stc[:, 24:24 + wd].bitcast(I32)
        em("vector", lambda e: e.tensor_single_scalar(yi, ti, 1, ALU.arith_shift_right),
           reads=[r_stc], writes=[r_stc])
        em("vector", lambda e: e.tensor_scalar(yi, yi, -1, 0x5f3759df, ALU.mult, ALU.add),
           reads=[r_stc], writes=[r_stc])
        for _ in range(3):
            em("vector", lambda e: e.tensor_mul(stc[:, 20:20 + wd], stc[:, 24:24 + wd],
                                                stc[:, 24:24 + wd]),
               reads=[r_stc], writes=[r_stc])
            em("vector", lambda e: e.scalar_tensor_tensor(
                stc[:, 20:20 + wd], stc[:, 20:20 + wd], -0.5, stc[:, 4:4 + wd],
                ALU.mult, ALU.mult),
               reads=[r_stc], writes=[r_stc])
            em("vector", lambda e: e.tensor_scalar_add(stc[:, 20:20 + wd], stc[:, 20:20 + wd], 1.5),
               reads=[r_stc], writes=[r_stc])
            em("vector", lambda e: e.tensor_mul(stc[:, 24:24 + wd], stc[:, 24:24 + wd],
                                                stc[:, 20:20 + wd]),
               reads=[r_stc], writes=[r_stc])

    globals()
    AW1S = [8192 + 1024 * i for i in range(12)]
    r_w1s = [Rs(f"w1s{i}") for i in range(12)]
    AWO[0], AWO[1] = 20480, 21504
    AIP = [22528, 23040]
    r_intp = [Rs("intp0"), Rs("intp1")]
    int_scr = nc.dram_tensor("int_scr", [32, 128, 2048], BF16, kind="Internal").ap()
    r_iscr = [[Rs(f"iscr{i}_{s}") for s in range(4)] for i in range(32)]
    w1scr = nc.dram_tensor("w1_scr", [128, 128, 1024], BF16, kind="Internal").ap()
    r_w1all = Rs("w1all")
    r_yscr = [Rs(f"yscr{i}") for i in range(NB)]


    def emit_a0x(blk):
        p3 = blk % 3
        xt, r_xt = x_tm[p3], r_x_tm[p3]
        em("sync", lambda e, blk=blk, xt=xt: e.dma_start(
            out=xt if xt.__class__.__name__ == "AP" else xt[:],
            in_=x_d[128 * blk:128 * blk + 128, :]),
           writes=[r_xt], dma="x%d" % p3)

    r_fsum = [Rs("fsum0"), Rs("fsum1")]
    r_fsq = [Rs("fsq0"), Rs("fsq1")]
    xl2 = [scrB[:, 0:1024], scrB[:, 1024:2048]]
    r_xl2 = [r_xl, r_xlB]

    def emit_a0a_sums(blk):
        par = blk % 2
        p3 = blk % 3
        xt, r_xt = x_tm[p3], r_x_tm[p3]
        em("scalar", lambda e, xt=xt, par=par: e.activation(t1_t, xt[:], AF.Identity,
                                                            accum_out=stc[:, 28 + par:29 + par]),
           reads=[r_xt], writes=[r_t1, r_fsum[par]])
        em("scalar", lambda e, xt=xt, par=par: e.activation(t2_t, xt[:], AF.Square,
                                                            accum_out=stc[:, 16 + par:17 + par]),
           reads=[r_xt], writes=[r_t2, r_fsq[par]])

    def emit_a0a_fin(b0):
        em("vector", lambda e: e.tensor_scalar_mul(stc[:, 32:34], stc[:, 28:30], 1.0 / D),
           reads=[r_fsum[0], r_fsum[1]], writes=[r_stc])
        em("vector", lambda e: e.tensor_scalar(stc[:, 4:6], stc[:, 16:18], 1.0 / D, LN_EPS,
                                               ALU.mult, ALU.add),
           reads=[r_fsq[0], r_fsq[1]], writes=[r_stc])
        em("vector", lambda e: e.tensor_mul(stc[:, 20:22], stc[:, 32:34], stc[:, 32:34]),
           reads=[r_stc], writes=[r_stc])
        em("vector", lambda e: e.tensor_sub(stc[:, 4:6], stc[:, 4:6], stc[:, 20:22]),
           reads=[r_stc], writes=[r_stc])
        rsqrt_batch(2)
        em("vector", lambda e, b0=b0: e.tensor_copy(lnmu[:, b0:b0 + 2], stc[:, 32:34]),
           reads=[r_stc], writes=[r_lnmu])
        em("vector", lambda e, b0=b0: e.tensor_copy(lnis[:, b0:b0 + 2], stc[:, 24:26]),
           reads=[r_stc], writes=[r_lnis])
        em("vector", lambda e: e.scalar_tensor_tensor(
            stc[:, 36:38], stc[:, 32:34], -1.0, stc[:, 24:26], ALU.mult, ALU.mult),
           reads=[r_stc], writes=[r_stc])

    def emit_a0ap(blk, xb, r_xb):
        par = blk % 2
        p3 = blk % 3
        xt, r_xt = x_tm[p3], r_x_tm[p3]
        em("scalar", lambda e, xt=xt, par=par, xb=xb: e.activation(
            xb, xt[:], AF.Identity, scale=stc[:, 24 + par:25 + par],
            bias=stc[:, 36 + par:37 + par]),
           reads=[r_xt, r_stc], writes=[r_xb])

    def emit_a0b(blk, xb, r_xb):
        par = blk % 2
        xlp, r_xlp = xl2[par], r_xl2[par]
        pt = (pP[2], pP[3]) if par == 0 else (pP[0], pP[1])
        r_pt = (r_pP[2], r_pP[3]) if par == 0 else (r_pP[0], r_pP[1])
        for c in range(8):
            tgt, r_tgt = (pt[0], r_pt[0]) if c < 4 else (pt[1], r_pt[1])
            em("tensor", lambda e, c=c, tgt=tgt, xb=xb: e.transpose(
                tgt[:, 128 * (c % 4):128 * (c % 4) + 128], xb[:, 128 * c:128 * c + 128], ident[:]),
               reads=[r_xb, r_ident], writes=[r_tgt])
        for half in range(2):
            src, r_src = (pt[0], r_pt[0]) if half == 0 else (pt[1], r_pt[1])
            dst = AP(xn_fm[:].tensor, 128 * blk + 2048 * 4 * half,
                     [[8 * T, 128], [2048, 4], [1, 128]])
            em("vector", lambda e, src=src, dst=dst: e.tensor_copy(dst, src[:]),
               reads=[r_src], writes=[r_xn])
            xnsl = AP(xn_fm[:].tensor, 128 * blk + 2048 * 4 * half,
                      [[8 * T, 128], [2048, 4], [1, 128]])
            em("vector", lambda e, src=src, xnsl=xnsl, half=half, xlp=xlp: e.tensor_sub(
                xlp[:, 512 * half:512 * half + 512], src[:], xnsl),
               reads=[r_src, r_xn], writes=[r_xlp])

    def emit_a0c_mm(blk, half):
        par = blk % 2
        xlp, r_xlp = xl2[par], r_xl2[par]
        for grp, (wt, r_wt, mov_xl) in enumerate(((wrh, r_wrh, False), (wrl, r_wrl, False),
                                                  (wrh, r_wrh, True))):
            for c8 in range(8):
                mv = xlp[:, 128 * c8:128 * c8 + 128] if mov_xl else xn_s(c8, 128 * blk, 128)
                r_mv = r_xlp if mov_xl else r_xn
                em("tensor", lambda e, c8=c8, wt=wt, mv=mv, grp=grp, half=half: e.matmul(
                    pAux[0:4, 128 * half:128 * half + 128], wt[:, 4 * c8:4 * c8 + 4], mv,
                    start=(grp == 0 and c8 == 0), stop=(grp == 2 and c8 == 7)),
                   reads=[r_wt, r_mv], writes=[r_pAux])

    def emit_a0c_cas(b0):
        for half in range(2):
            em("vector", lambda e, half=half: e.tensor_copy(
                lg_f[:], pAux[0:4, 128 * half:128 * half + 128]),
               reads=[r_pAux], writes=[r_lg_f])
            em("tensor", lambda e, half=half: e.transpose(
                pAux[:, 256 + 4 * half:260 + 4 * half], lg_f[:],
                ident[0:4, 0:4]),
               reads=[r_lg_f, r_ident], writes=[r_pAux])
            em("vector", lambda e, half=half: e.tensor_copy(
                lg_tm[:, 4 * half:4 * half + 4], pAux[:, 256 + 4 * half:260 + 4 * half]),
               reads=[r_pAux], writes=[r_lg_tm])
            em("vector", lambda e, half=half: e.reduce_max(
                stc[:, 12 + half:13 + half], lg_tm[:, 4 * half:4 * half + 4], axis=AX.X),
               reads=[r_lg_tm], writes=[r_stc])
            em("vector", lambda e, half=half: e.tensor_scalar(
                lg_tm[:, 4 * half:4 * half + 4], lg_tm[:, 4 * half:4 * half + 4],
                stc[:, 12 + half:13 + half], None, ALU.is_ge),
               reads=[r_stc, r_lg_tm], writes=[r_lg_tm])

        def eqp(j):
            return AP(lg_tm[:].tensor, j, [[8, 128], [4, 2]])

        def mp(j):
            return AP(mask_tm[:].tensor, 4 * b0 + j, [[4 * NB, 128], [4, 2]])

        em("vector", lambda e: e.tensor_copy(mp(0), eqp(0)),
           reads=[r_lg_tm], writes=[r_mask_tm])
        em("vector", lambda e: e.tensor_copy(alr_t[:, 0:2], eqp(0)),
           reads=[r_lg_tm], writes=[r_alr_t])
        for j in (1, 2, 3):
            em("vector", lambda e, j=j: e.tensor_tensor(
                alr_t[:, 2:4], eqp(j), alr_t[:, 0:2], ALU.mult),
               reads=[r_lg_tm, r_alr_t], writes=[r_alr_t])
            em("vector", lambda e, j=j: e.tensor_sub(mp(j), eqp(j), alr_t[:, 2:4]),
               reads=[r_lg_tm, r_alr_t], writes=[r_mask_tm])
            if j < 3:
                em("vector", lambda e, j=j: e.tensor_add(alr_t[:, 0:2], alr_t[:, 0:2], mp(j)),
                   reads=[r_mask_tm, r_alr_t], writes=[r_alr_t])

    qb16 = q_t.bitcast(BF16)[:, 0:1024]
    t1b16 = t1_t.bitcast(BF16)[:, 0:1024]
    t2b16 = t2_t.bitcast(BF16)[:, 0:1024]

    def piece_pipeline(blk, pbank0, pbank1, r_pb0, r_pb1, bias_off, gb, r_gb, dst_ap, r_dst):
        for half, (pbk, r_pbk) in enumerate(((pbank0, r_pb0), (pbank1, r_pb1))):
            em("vector", lambda e, h=half, pbk=pbk: e.tensor_add(
                qb16[:, 512 * h:512 * h + 512], pbk[:],
                bqb[:, bias_off + 512 * h: bias_off + 512 * h + 512]),
               reads=[r_pbk, r_bqb], writes=[r_q_t])
        em("scalar", lambda e: e.activation(t2_t, qb16, AF.Square, accum_out=stc[:, 1:2]),
           reads=[r_q_t], writes=[r_t2, r_stc])
        em("vector", lambda e: e.tensor_scalar(stc[:, 4:5], stc[:, 1:2], 1.0 / D, RMS_EPS,
                                               ALU.mult, ALU.add),
           reads=[r_stc], writes=[r_stc])
        rsqrt_steps()
        em("vector", lambda e, gb=gb: e.tensor_mul(qb16, qb16, gb[:]),
           reads=[r_q_t, r_gb], writes=[r_q_t])
        crep = AP(cos_all[:].tensor, 32 * blk, [[NB * 32, 128], [0, 16], [0, 2], [1, 32]])
        srep = AP(spair[:].tensor, 64 * blk, [[NB * 64, 128], [0, 16], [1, 64]])
        qswap = AP(qb16.tensor, qb16.offset + 32, [[8192, 128], [64, 16], [-32, 2], [1, 32]])
        em("vector", lambda e, crep=crep: e.tensor_tensor(t1b16, qb16, crep, ALU.mult),
           reads=[r_q_t, r_cos], writes=[r_t1])
        em("vector", lambda e, qswap=qswap, srep=srep: e.tensor_tensor(t2b16, qswap, srep, ALU.mult),
           reads=[r_q_t, r_spair], writes=[r_t2])
        em("vector", lambda e: e.tensor_add(t1b16, t1b16, t2b16),
           reads=[r_t1, r_t2], writes=[r_t1])
        em("vector", lambda e, dst_ap=dst_ap: e.tensor_scalar(
            dst_ap, t1b16, stc[:, 9:10], 0.0, ALU.mult, ALU.max),
           reads=[r_t1, r_stc], writes=[r_dst])

    def emit_a1a(blk):
        for c8 in range(8):
            for half in range(2):
                tgt, r_tgt = (pP[2], r_pP[2]) if half == 0 else (pP[3], r_pP[3])
                em("tensor", lambda e, c8=c8, blk=blk, h=half, tgt=tgt: e.matmul(
                    tgt[:], xn_s(c8, 128 * blk, 128), ar(AQ + 1024 * c8 + 512 * h, 512),
                    start=(c8 == 0), stop=(c8 == 7)),
                   reads=[r_xn, r_wq], writes=[r_tgt])
        piece_pipeline(blk, pP[2], pP[3], r_pP[2], r_pP[3], 0, gqb, r_gqb, t2_t, r_t2)

    def emit_a1b(blk):
        for wave in range(2):
            tgt, r_tgt = (pP[2], r_pP[2]) if wave == 0 else (pP[3], r_pP[3])
            for c in range(4):
                cc = 4 * wave + c
                em("tensor", lambda e, cc=cc, c=c, tgt=tgt: e.transpose(
                    tgt[:, 128 * c:128 * c + 128], t2_t[:, 128 * cc:128 * cc + 128], ident[:]),
                   reads=[r_t2, r_ident], writes=[r_tgt])
            dst = AP(qf_fm[:].tensor, 128 * blk + 2048 * 4 * wave,
                     [[8 * T, 128], [2048, 4], [1, 128]])
            em("vector", lambda e, dst=dst, tgt=tgt: e.tensor_copy(dst, tgt[:]),
               reads=[r_tgt], writes=[r_qf])

    xb316 = xn_t.bitcast(BF16)[:, 0:1024]

    def a2_kmms(blk, pa, pb_, r_pa, r_pb):
        for c8 in range(8):
            for half, (tgt, r_tgt) in enumerate(((pa, r_pa), (pb_, r_pb))):
                em("tensor", lambda e, c8=c8, blk=blk, h=half, tgt=tgt: e.matmul(
                    tgt[:], xn_s(c8, 128 * blk, 128), ar(AK + 1024 * c8 + 512 * h, 512),
                    start=(c8 == 0), stop=(c8 == 7)),
                   reads=[r_xn, r_wk], writes=[r_tgt])

    def piece_sums(pb0, pb1, r0, r1, qv, r_qv, par):
        for half, (pbk, r_pbk) in enumerate(((pb0, r0), (pb1, r1))):
            em("vector", lambda e, h=half, pbk=pbk, qv=qv: e.tensor_add(
                qv[:, 512 * h:512 * h + 512], pbk[:],
                bqb[:, D + 512 * h: D + 512 * h + 512]),
               reads=[r_pbk, r_bqb], writes=[r_qv])
        em("scalar", lambda e, qv=qv, par=par: e.activation(t2_t, qv, AF.Square,
                                                            accum_out=stc[:, 16 + par:17 + par]),
           reads=[r_qv], writes=[r_t2, r_fsq[par]])

    def piece_batfin():
        em("vector", lambda e: e.tensor_scalar(stc[:, 4:6], stc[:, 16:18], 1.0 / D, RMS_EPS,
                                               ALU.mult, ALU.add),
           reads=[r_fsq[0], r_fsq[1]], writes=[r_stc])
        rsqrt_batch(2)

    def piece_fin(blk, qv, r_qv, dst_ap, r_dst, par):
        em("vector", lambda e, qv=qv: e.tensor_mul(qv, qv, gkb[:]),
           reads=[r_qv, r_gkb], writes=[r_qv])
        crep = AP(cos_all[:].tensor, 32 * blk, [[NB * 32, 128], [0, 16], [0, 2], [1, 32]])
        srep = AP(spair[:].tensor, 64 * blk, [[NB * 64, 128], [0, 16], [1, 64]])
        qswap = AP(qv.tensor, qv.offset + 32, [[8192, 128], [64, 16], [-32, 2], [1, 32]])
        em("vector", lambda e, crep=crep, qv=qv: e.tensor_tensor(t1b16, qv, crep, ALU.mult),
           reads=[r_qv, r_cos], writes=[r_t1])
        em("vector", lambda e, qswap=qswap, srep=srep: e.tensor_tensor(t2b16, qswap, srep, ALU.mult),
           reads=[r_qv, r_spair], writes=[r_t2])
        em("vector", lambda e: e.tensor_add(t1b16, t1b16, t2b16),
           reads=[r_t1, r_t2], writes=[r_t1])
        em("vector", lambda e, dst_ap=dst_ap, par=par: e.tensor_scalar(
            dst_ap, t1b16, stc[:, 24 + par:25 + par], 0.0, ALU.mult, ALU.max),
           reads=[r_t1, r_stc], writes=[r_dst])

    def a2_vmms_evac(blk, pc, pd_, r_pc, r_pd, par):
        for c8 in range(8):
            for half, (tgt, r_tgt) in enumerate(((pc, r_pc), (pd_, r_pd))):
                em("tensor", lambda e, c8=c8, blk=blk, h=half, tgt=tgt: e.matmul(
                    tgt[:], xn_s(c8, 128 * blk, 128), ar(AV + 1024 * c8 + 512 * h, 512),
                    start=(c8 == 0), stop=(c8 == 7)),
                   reads=[r_xn, r_wv], writes=[r_tgt])
        vt, r_vtr = v_t[par], r_v[par]
        for half, (pbk, r_pbk) in enumerate(((pc, r_pc), (pd_, r_pd))):
            em("vector", lambda e, h=half, pbk=pbk, vt=vt: e.tensor_add(
                vt[:, 512 * h:512 * h + 512], pbk[:],
                bqb[:, 2 * D + 512 * h: 2 * D + 512 * h + 512]),
               reads=[r_pbk, r_bqb], writes=[r_vtr])

    def emit_a2_vk(blk):
        par = blk % 2
        kf, r_kfr = kf_t[par], r_kf[par]
        vt, r_vtr = v_t[par], r_v[par]
        for p in range(8):
            bank = p // 3 if p < 6 else 2
            off = (p % 3) * 129 if p < 6 else (p - 6) * 129
            pv, r_pv = pVK[bank], r_pVK[bank]
            first = (blk == 0 and (p % 3 == 0 if p < 6 else p == 6))
            last = (blk == NB - 1 and p in (2, 5, 7))
            em("tensor", lambda e, p=p, off=off, pv=pv, kf=kf, vt=vt, first=first: e.matmul(
                pv[:, off:off + 64], kf[:, 128 * p:128 * p + 128], vt[:, 128 * p:128 * p + 64],
                start=first, stop=False),
               reads=[r_kfr, r_vtr], writes=[r_pv])
            em("tensor", lambda e, p=p, off=off, pv=pv, kf=kf: e.matmul(
                pv[:, off + 64:off + 65], kf[:, 128 * p:128 * p + 128], onescol[:],
                start=False, stop=False),
               reads=[r_kfr, r_onescol], writes=[r_pv])
            em("tensor", lambda e, p=p, off=off, pv=pv, kf=kf, vt=vt, last=last: e.matmul(
                pv[:, off + 65:off + 129], kf[:, 128 * p:128 * p + 128],
                vt[:, 128 * p + 64:128 * p + 128],
                start=False, stop=last),
               reads=[r_kfr, r_vtr], writes=[r_pv])

    M16_OFF = [0, 512, 2048, 2560, 3072, 3584, 4096, 4608,
               5120, 5632, 7680, 8192, 8704, 9216, 9728, 10240]
    r_m16 = [Rs(f"m16_{i}") for i in range(16)]

    def m16_ap(i):
        return scrB[:, M16_OFF[i]:M16_OFF[i] + 512]

    def emit_masks():
        alias(r_m16[0], [r_xl])
        alias(r_m16[1], [r_xl])
        alias(r_m16[2], [r_kf[1]])
        alias(r_m16[3], [r_kf[1]])
        alias(r_m16[4], [r_v[0]])
        alias(r_m16[5], [r_v[0]])
        alias(r_m16[6], [r_v[1], r_maskb])
        alias(r_m16[7], [r_v[1], r_maskb])
        alias(r_m16[8], [r_maskb])
        alias(r_m16[9], [r_maskb])
        alias(r_m16[11], [r_y[0]])
        alias(r_m16[12], [r_y[0]])
        alias(r_m16[13], [r_y[1], r_rowbf])
        alias(r_m16[14], [r_y[1], r_rowbf])
        alias(r_m16[15], [r_y[2], r_rowbf])
        alias(r_mrows, [r_mrows, r_gx3])
        for s in range(NS):
            for b4 in range(4):
                blk = 4 * s + b4
                em("tensor", lambda e, blk=blk, b4=b4: e.transpose(
                    pAux[0:4, 128 * b4:128 * b4 + 128], mask_tm[:, 4 * blk:4 * blk + 4], ident[:]),
                   reads=[r_mask_tm, r_ident], writes=[r_pAux])
            em("vector", lambda e: e.tensor_copy(mrows, pAux[0:4, 0:512]),
               reads=[r_pAux], writes=[r_mrows])
            for ex in range(4):
                em("tensor", lambda e, ex=ex: e.matmul(
                    pAux[:], erows[:, 128 * ex:128 * ex + 128], mrows, start=True, stop=True),
                   reads=[r_erows, r_mrows], writes=[r_pAux])
                em("scalar", lambda e, s=s, ex=ex: e.activation(
                    m16_ap(4 * s + ex), pAux[:], AF.Copy),
                   reads=[r_pAux], writes=[r_m16[4 * s + ex]])
        alias(r_gx3, [r_gx3, r_mrows])

    def w1_dma(ich):
        ph3 = ich % 3
        em("gpsimd", lambda e, ich=ich, ph3=ph3: e.dma_start(
            out=ar(8192 + ph3 * 4096, 4096),
            in_=AP(w1_d.tensor, ich * 131072,
                   [[1024, 128], [4194304, 4], [1, 1024]])),
           writes=[r_w1s[ph3]], dma="w1g%d" % ph3)

    def emit_ffn_h(inject):
        w1_dma(0)
        w1_dma(1)
        for ich in range(32):
            if ich + 2 < 32:
                w1_dma(ich + 2)
            slot = ich % 3
            for s in range(NS):
                S0 = ST * s
                ip = (4 * ich + s) % 2
                for ex in range(4):
                    hb = (ich * 16 + s * 4 + ex) % 5
                    ph = [pP[0], pP[1], pVK[0], pVK[1], pVK[2]][hb]
                    r_ph = [r_pP[0], r_pP[1], r_pVK[0], r_pVK[1], r_pVK[2]][hb]
                    for c8 in range(8):
                        em("tensor", lambda e, c8=c8, ex=ex, slot=slot, S0=S0, ph=ph: e.matmul(
                            ph[:], ar(8192 + slot * 4096 + ex * 1024 + 128 * c8, 128),
                            xn_s(c8, S0, ST), start=(c8 == 0), stop=(c8 == 7)),
                           reads=[r_w1s[slot], r_xn], writes=[r_ph])
                    gp_, r_gp = ((g_sb[0], r_g[0]), (g_sb[1], r_g[1]),
                                 (g_x2, r_gx2), (g_x3, r_gx3))[ex]
                    em("scalar", lambda e, ex=ex, ich=ich, ph=ph, gp_=gp_: e.activation(
                        gp_, ph[:], AF.Gelu, bias=b1t_sb[:, 32 * ex + ich:32 * ex + ich + 1]),
                       reads=[r_ph, r_b1t], writes=[r_gp])
                    mi = 4 * s + ex
                    if ex == 0:
                        em("vector", lambda e, ip=ip, gp_=gp_, mi=mi: e.tensor_mul(
                            ar(AIP[ip], 512), gp_, m16_ap(mi)),
                           reads=[r_gp, r_m16[mi]], writes=[r_intp[ip]])
                    else:
                        mc, r_mc = (mac_t, r_mac) if ex % 2 else (mac2, r_mac2)
                        em("vector", lambda e, gp_=gp_, mc=mc, mi=mi: e.tensor_mul(
                            mc, gp_, m16_ap(mi)),
                           reads=[r_m16[mi], r_gp], writes=[r_mc])
                        em("vector", lambda e, ip=ip, mc=mc: e.tensor_add(
                            ar(AIP[ip], 512), ar(AIP[ip], 512), mc),
                           reads=[r_mc, r_intp[ip]], writes=[r_intp[ip]])
                em("sync", lambda e, ich=ich, s=s, ip=ip: e.dma_start(
                    out=AP(int_scr.tensor, ich * 262144 + 512 * s, [[2048, 128], [1, 512]]),
                    in_=ar(AIP[ip], 512)),
                   reads=[r_intp[ip]], writes=[r_iscr[ich][s]], dma="is%d" % ip)
            if inject:
                inject.pop(0)()
        while inject:
            inject.pop(0)()

    pY8 = [pP[0], pP[1], pP[2], pP[3], pVK[0], pVK[1], pVK[2], pAux]
    r_pY8 = [r_pP[0], r_pP[1], r_pP[2], r_pP[3], r_pVK[0], r_pVK[1], r_pVK[2], r_pAux]

    WOBQ = [0, 4096, 8192, 12288]
    AIPQ = [16384, 18432, 20480, 22528]
    r_wob4 = [Rs(f"wob4_{i}") for i in range(4)]
    r_yip4 = [Rs(f"yip4_{i}") for i in range(4)]

    def emit_ffn_y(s):
        if s == 0:
            alias(r_wob4[0], [r_wq])
            alias(r_wob4[1], [r_wq])
            alias(r_wob4[2], [r_w1s[0]])
            alias(r_wob4[3], [r_w1s[1]])
            alias(r_yip4[0], [r_w1s[2]])
            alias(r_yip4[1], [r_w1s[2]])
            alias(r_yip4[2], [r_wob[0], r_wob[1], r_x_tm[2]])
            alias(r_yip4[3], [r_intp[0], r_intp[1]])
        for q in range(8):
            sl = q % 4
            weng, wstream = (("scalar", "wob%d" % sl) if sl < 2
                             else ("gpsimd", "wobg%d" % (sl - 2)))
            em(weng, lambda e, q=q, sl=sl: e.dma_start(
                out=ar(WOBQ[sl], 4096),
                in_=AP(wo_bf_d.tensor, q * 524288, [[1024, 128], [131072, 4], [1, 1024]])),
               reads=[r_wobf[q // 2]], writes=[r_wob4[sl]], dma=wstream)
            em("sync", lambda e, q=q, s=s, sl=sl: e.dma_start(
                out=ar(AIPQ[sl], 2048),
                in_=AP(int_scr.tensor, q * 1048576 + 512 * s, [[2048, 128], [262144, 4], [1, 512]])),
               reads=[r_iscr[4 * q + j][s] for j in range(4)],
               writes=[r_yip4[sl]], dma="ir%d" % sl)
            for m in range(4):
                ich = 4 * q + m
                for b4 in range(4):
                    for hf in range(2):
                        em("tensor", lambda e, b4=b4, hf=hf, m=m, ich=ich, sl=sl: e.matmul(
                            pY8[2 * b4 + hf][:], ar(AIPQ[sl] + 512 * m + 128 * b4, 128),
                            ar(WOBQ[sl] + 1024 * m + 512 * hf, 512),
                            start=(ich == 0), stop=(ich == 31)),
                           reads=[r_yip4[sl], r_wob4[sl]], writes=[r_pY8[2 * b4 + hf]])
        for b4 in range(4):
            for hf in range(2):
                em("vector", lambda e, b4=b4, hf=hf: e.tensor_add(
                    y_sb[b4][:, 512 * hf:512 * hf + 512], pY8[2 * b4 + hf][:],
                    bob[:, 512 * hf:512 * hf + 512]),
                   reads=[r_pY8[2 * b4 + hf], r_bob], writes=[r_y[b4]])
        for b4 in range(4):
            blk = 4 * s + b4
            em("sync", lambda e, blk=blk, b4=b4: e.dma_start(
                out=yscr_d[128 * blk:128 * blk + 128, :], in_=y_sb[b4]),
               reads=[r_y[b4]], writes=[r_yscr[blk]], dma="ys")

    em("gpsimd", lambda e: e.dma_start(
        out=wr_f, in_=AP(wr_d.tensor, 0, [[4, 128], [512, 8], [1, 4]])),
       writes=[r_wr_f], dma="row")
    em("vector", lambda e: e.tensor_copy(wrh[:], wr_f), reads=[r_wr_f], writes=[r_wrh])
    em("vector", lambda e: e.tensor_sub(wrl[:], wr_f, wrh[:]),
       reads=[r_wr_f, r_wrh], writes=[r_wrl])
    em("gpsimd", lambda e: e.dma_start(
        out=b1t_sb[:], in_=AP(b1t_d.tensor, 0, [[32, 128], [4096, 4], [1, 32]])),
       writes=[r_b1t], dma="row")
    em("gpsimd", lambda e: e.dma_start(
        out=woutT_sb[:], in_=AP(woutT_d.tensor, 0, [[1024, 128], [131072, 8], [1, 1024]])),
       writes=[r_wout], dma="row")

    em("vector", lambda e: e.tensor_copy(rop1, iot1[:, 0:32]), reads=[r_iot1], writes=[r_rop1])
    em("scalar", lambda e: e.activation(invf2[:], rop1, AF.Exp, scale=-math.log(THETA) / 32.0),
       reads=[r_rop1], writes=[r_invf2])
    em("vector", lambda e: e.tensor_scalar_mul(invf2[:], invf2[:], 1.0 / (2 * math.pi)),
       reads=[r_invf2], writes=[r_invf2])
    def emit_ropes_all():
        ropP = scrF[:, 3072:3584]
        ropA = scrF[:, 1024:1536]
        ropBi = scrF[:, 1536:2048].bitcast(I32)[:, 0:512]
        ropC = scrF[:, 2048:2560]
        ropD = scrF[:, 2560:3072]
        invf_bc = AP(invf2[:].tensor, 0, [[32, 128], [0, 16], [1, 32]])
        em("gpsimd", lambda e: e.iota(ropP, pattern=[[128, 16], [0, 32]], base=0,
                                      channel_multiplier=1,
                                      allow_small_or_imprecise_dtypes=True),
           writes=[rF[3]])
        for shift in (0.25, 0.0):
            em("vector", lambda e: e.tensor_tensor(ropA, ropP, invf_bc, ALU.mult),
               reads=[rF[3], r_invf2], writes=[rF[1]])
            if shift:
                em("vector", lambda e, sh=shift: e.tensor_scalar_add(ropA, ropA, sh),
                   reads=[rF[1]], writes=[rF[1]])
            em("vector", lambda e: e.tensor_copy(ropBi, ropA), reads=[rF[1]], writes=[rF[1]])
            em("vector", lambda e: e.tensor_copy(ropC, ropBi), reads=[rF[1]], writes=[rF[2]])
            em("vector", lambda e: e.tensor_sub(ropC, ropA, ropC),
               reads=[rF[1], rF[2]], writes=[rF[2]])
            em("vector", lambda e: e.tensor_scalar_mul(ropC, ropC, 2 * math.pi),
               reads=[rF[2]], writes=[rF[2]])
            em("vector", lambda e: e.tensor_single_scalar(ropD, ropC, math.pi, ALU.is_gt),
               reads=[rF[2]], writes=[rF[2]])
            em("vector", lambda e: e.scalar_tensor_tensor(ropC, ropD, -2 * math.pi, ropC,
                                                          ALU.mult, ALU.add),
               reads=[rF[2]], writes=[rF[2]])
            em("vector", lambda e: e.tensor_scalar(ropC, ropC, math.pi, -math.pi,
                                                   ALU.min, ALU.max),
               reads=[rF[2]], writes=[rF[2]])
            if shift:
                em("scalar", lambda e: e.activation(cos_all[:, 0:512], ropC, AF.Sin),
                   reads=[rF[2]], writes=[r_cos])
            else:
                em("scalar", lambda e: e.activation(ropA, ropC, AF.Sin),
                   reads=[rF[2]], writes=[rF[1]])
        sup = AP(spair[:].tensor, 32, [[NB * 64, 128], [64, 16], [1, 32]])
        slo = AP(spair[:].tensor, 0, [[NB * 64, 128], [64, 16], [1, 32]])
        em("vector", lambda e, sup=sup: e.tensor_copy(sup, ropA),
           reads=[rF[1]], writes=[r_spair])
        em("vector", lambda e, slo=slo: e.tensor_scalar_mul(slo, ropA, -1.0),
           reads=[rF[1]], writes=[r_spair])

    def bcast_build(row_dram, ncols, dst, r_dst):
        em("gpsimd", lambda e: e.dma_start(out=rowbf[0:1, 0:ncols], in_=row_dram),
           writes=[r_rowbf], dma="row")
        for j0 in range(0, ncols, 512):
            w = min(512, ncols - j0)
            em("tensor", lambda e, j0=j0, w=w: e.matmul(
                pVK[0][:, 0:w], ones1[:], rowbf[0:1, j0:j0 + w], start=True, stop=True),
               reads=[r_ones1, r_rowbf], writes=[r_pVK[0]])
            em("scalar", lambda e, j0=j0, w=w, dst=dst: e.activation(
                dst[:, j0:j0 + w], pVK[0][:, 0:w], AF.Copy),
               reads=[r_pVK[0]], writes=[r_dst])

    def emit_bcasts():
        bcast_build(bqkv_d, 3 * D, bqb, r_bqb)
        bcast_build(gq_d, D, gqb, r_gqb)
        bcast_build(gk_d, D, gkb, r_gkb)
        bcast_build(bout_d, D, boutb, r_boutb)
        bcast_build(gout_d, D, goutb, r_goutb)
        bcast_build(bo_d, D, bob, r_bob)
        bcast_build(gamma_d, D, gammab, r_gammab)
        bcast_build(beta_d, D, betab, r_betab)
        for _rs in (r_y[1], r_y[2], r_y[3]):
            alias(_rs, [_rs, r_rowbf])

    r_wobf = [Rs(f"wobf{c}") for c in range(4)]
    for c in range(4):
        em("gpsimd", lambda e, c=c: e.dma_start(
            out=AP(wo_bf_d.tensor, c * 1048576, [[1024, 1024], [1, 1024]]),
            in_=AP(wo_d.tensor, c * 1048576, [[1024, 1024], [1, 1024]])),
           writes=[r_wobf[c]], dma=("wk", "wv", "wobg0", "wobg1")[c])
    for b0 in range(0, NB, 2):
        emit_a0x(b0)
        emit_a0x(b0 + 1)
        emit_a0a_sums(b0)
        emit_a0a_sums(b0 + 1)
        emit_a0a_fin(b0)
        for blk in (b0, b0 + 1):
            xb, r_xb = (xn_t, rF[3]) if blk % 2 == 0 else (t1_t, rF[0])
            emit_a0ap(blk, xb, r_xb)
            emit_a0b(blk, xb, r_xb)
            emit_a0c_mm(blk, blk - b0)
        emit_a0c_cas(b0)
        if b0 == 12:
            emit_bcasts()
    emit_ropes_all()
    emit_masks()
    alias(r_mac2, [r_mac2, r_xlB])
    alias(r_gx2, [r_gx2, r_xlB])
    inj = []
    for blk in range(NB):
        inj.append(lambda b=blk: emit_a1a(b))
        inj.append(lambda b=blk: emit_a1b(b))
    emit_ffn_h(inj)
    alias(r_y[0], [r_y[0], r_m16[11], r_m16[12]])
    alias(r_y[1], [r_y[1], r_m16[13], r_m16[14]])
    alias(r_y[2], [r_y[2], r_m16[15]])
    for s in range(NS):
        emit_ffn_y(s)

    alias(r_wk, [r_w1s[0], r_w1s[1], r_wob4[2], r_wob4[3]])
    alias(r_kf[0], [r_kf[0], r_mac2, r_gx2])
    alias(r_kf[1], [r_kf[1], r_m16[2], r_m16[3]])
    alias(r_v[0], [r_v[0], r_m16[4], r_m16[5]])
    alias(r_wv, [r_w1s[2], r_wob[0], r_wob[1], r_intp[0], r_intp[1], r_x_tm[2],
                 r_yip4[0], r_yip4[1], r_yip4[2], r_yip4[3]])
    em("gpsimd", lambda e: e.dma_start(
        out=ar(AK, 8192), in_=AP(wqkvT_d.tensor, 1024, [[3072, 128], [393216, 8], [1, 1024]])),
       writes=[r_wk], dma="wk")
    em("gpsimd", lambda e: e.dma_start(
        out=ar(AV, 8192), in_=AP(wqkvT_d.tensor, 2048, [[3072, 128], [393216, 8], [1, 1024]])),
       writes=[r_wv], dma="wv")
    alias(r_v[1], [r_maskb, r_m16[6], r_m16[7]])
    a2_kmms(0, pP[0], pP[1], r_pP[0], r_pP[1])
    a2_kmms(1, pP[2], pP[3], r_pP[2], r_pP[3])
    piece_sums(pP[0], pP[1], r_pP[0], r_pP[1], qb16, r_q_t, 0)
    piece_sums(pP[2], pP[3], r_pP[2], r_pP[3], xb316, rF[3], 1)
    piece_batfin()
    for b0 in range(0, NB, 2):
        piece_fin(b0, qb16, r_q_t, kf_t[0], r_kf[0], 0)
        piece_fin(b0 + 1, xb316, rF[3], kf_t[1], r_kf[1], 1)
        a2_vmms_evac(b0, pP[0], pP[1], r_pP[0], r_pP[1], 0)
        a2_vmms_evac(b0 + 1, pP[2], pP[3], r_pP[2], r_pP[3], 1)
        if b0 + 2 < NB:
            a2_kmms(b0 + 2, pP[0], pP[1], r_pP[0], r_pP[1])
            a2_kmms(b0 + 3, pP[2], pP[3], r_pP[2], r_pP[3])
            piece_sums(pP[0], pP[1], r_pP[0], r_pP[1], qb16, r_q_t, 0)
            piece_sums(pP[2], pP[3], r_pP[2], r_pP[3], xb316, rF[3], 1)
            piece_batfin()
        emit_a2_vk(b0)
        emit_a2_vk(b0 + 1)

    for p in range(8):
        bank = p // 3 if p < 6 else 2
        off = (p % 3) * 129 if p < 6 else (p - 6) * 129
        pv, r_pv = pVK[bank], r_pVK[bank]
        em("vector", lambda e, p=p, off=off, pv=pv: e.tensor_copy(
            vkN[0:64, 128 * p:128 * p + 64], pv[0:64, off:off + 64]),
           reads=[r_pv], writes=[r_vkN])
        em("vector", lambda e, p=p, off=off, pv=pv: e.tensor_copy(
            vkN[64:128, 128 * p + 64:128 * p + 128], pv[64:128, off + 65:off + 129]),
           reads=[r_pv], writes=[r_vkN])
        em("vector", lambda e, p=p, off=off, pv=pv: e.tensor_copy(
            sfull[0:64, 18 * p:18 * p + 1], pv[0:64, off + 64:off + 65]),
           reads=[r_pv], writes=[r_sfull])
        em("vector", lambda e, p=p, off=off, pv=pv: e.tensor_copy(
            sfull[64:128, 18 * p + 1:18 * p + 2], pv[64:128, off + 64:off + 65]),
           reads=[r_pv], writes=[r_sfull])

    r_attn8 = [Rs(f"attn8_{c}") for c in range(8)]
    for c in range(8):
        alias(r_attn8[c], [r_attn, r_xl, r_kf[0], r_kf[1], r_v[0], r_v[1], r_rowbf]
              + r_m16[0:10])

    arenaF = arena[:, 0:24576].bitcast(F32)
    xnF = xn_fm[:, 0:16384].bitcast(F32)
    zb_t = [arena[:, 1024 * i:1024 * i + 1024] for i in range(4)]
    xnb_t = [arena[:, 4096 + 1024 * i:4096 + 1024 * i + 1024] for i in range(4)]
    junkF4 = [arenaF[:, 4096 + 1024 * i:4096 + 1024 * i + 1024] for i in range(4)]
    yb_t = [arena[:, 16384 + 1024 * i:16384 + 1024 * i + 1024] for i in range(4)]
    x4 = [[xnF[:, 4096 * h + 1024 * i:4096 * h + 1024 * i + 1024] for i in range(4)]
          for h in range(2)]
    r_zb = [Rs(f"tzb{i}") for i in range(4)]
    r_xnb = [Rs(f"txnb{i}") for i in range(4)]
    r_junkq = [Rs(f"tjunk{i}") for i in range(4)]
    r_yb = [Rs(f"tyb{i}") for i in range(4)]
    r_x4 = [[Rs(f"tx4_{h}{i}") for i in range(4)] for h in range(2)]
    r_stq = [Rs(f"stq{i}") for i in range(4)]
    r_sty = [Rs(f"sty{i}") for i in range(4)]
    for i in range(4):
        alias(r_zb[i], [r_wq, r_wob4[0], r_wob4[1]])
        alias(r_xnb[i], [r_wq, r_wob4[0], r_wob4[1]])
        alias(r_junkq[i], [r_wk, r_wob4[2], r_wob4[3]])
        alias(r_yb[i], [r_wv, r_yip4[0], r_yip4[1]])
        alias(r_x4[0][i], [r_xn])
        alias(r_x4[1][i], [r_xn])

    alias(r_rd, [r_rd, r_maskb] + r_m16[6:10])
    alias(r_den, [r_den, r_gx3])
    for s in range(NS):
        S0 = ST * s
        for c8 in range(8):
            em("tensor", lambda e, c8=c8, S0=S0: e.matmul(
                pAux[0:16, :], sfull[:, 16 * c8:16 * c8 + 16], qf_s(c8, S0, ST),
                start=(c8 == 0), stop=(c8 == 7)),
               reads=[r_sfull, r_qf], writes=[r_pAux])
        em("vector", lambda e, s=s: e.tensor_scalar_add(
            rd_bf[:, 512 * s:512 * s + 512], pAux[0:16, :], EPS_LIN),
           reads=[r_pAux], writes=[r_rd])
    em("vector", lambda e: e.reciprocal(rd_bf, rd_bf), reads=[r_rd], writes=[r_rd])
    def emit_tail_front(s):
        S0 = ST * s
        h = s % 2
        for b4 in range(4):
            blk = 4 * s + b4
            em("sync", lambda e, blk=blk, h=h, b4=b4: e.dma_start(
                out=x4[h][b4], in_=x_d[128 * blk:128 * blk + 128, :]),
               writes=[r_x4[h][b4]], dma="x%d" % (b4 % 2))
        for c8 in range(8):
            prd, r_prd = pP[c8 % 2], r_pP[c8 % 2]
            pat, r_pat = pP[2 + c8 % 2], r_pP[2 + c8 % 2]
            em("tensor", lambda e, c8=c8, S0=S0, prd=prd: e.matmul(
                prd[:], selheads[:, 128 * c8:128 * c8 + 128], rd_bf[:, S0:S0 + ST],
                start=True, stop=True),
               reads=[r_selh, r_rd], writes=[r_prd])
            qp, r_qp = (qfp_sb, r_qfp) if c8 % 2 == 0 else (g_sb[0], r_g[0])
            em("vector", lambda e, c8=c8, S0=S0, prd=prd, qp=qp: e.tensor_mul(
                qp, prd[:], qf_s(c8, S0, ST)),
               reads=[r_prd, r_qf], writes=[r_qp])
            em("tensor", lambda e, c8=c8, pat=pat, qp=qp: e.matmul(
                pat[:], vkN[:, 128 * c8:128 * c8 + 128], qp, start=True, stop=True),
               reads=[r_vkN, r_qp], writes=[r_pat])
            em("scalar", lambda e, c8=c8, pat=pat: e.activation(
                attn_sb[:, 512 * c8:512 * c8 + 512], pat[:], AF.Copy),
               reads=[r_pat], writes=[r_attn8[c8]])

    ZBANKS = [(pVK[0], pVK[1]), (pVK[2], pAux), (pP[0], pP[1]), (pP[2], pP[3])]
    ZRBANKS = [(r_pVK[0], r_pVK[1]), (r_pVK[2], r_pAux), (r_pP[0], r_pP[1]),
               (r_pP[2], r_pP[3])]

    def emit_tail_z_mms(s):
        for b4 in range(4):
            pz = list(ZBANKS[b4])
            r_pz = list(ZRBANKS[b4])
            for c8 in range(8):
                for half in range(2):
                    em("tensor", lambda e, c8=c8, b4=b4, half=half, pz=pz: e.matmul(
                        pz[half][:], attn_sb[:, 512 * c8 + 128 * b4:512 * c8 + 128 * b4 + 128],
                        woutT_sb[:, 1024 * c8 + 512 * half:1024 * c8 + 512 * half + 512],
                        start=(c8 == 0), stop=(c8 == 7)),
                       reads=[r_attn8[c8], r_wout], writes=[r_pz[half]])

    def emit_tail_z_evac(s):
        for b4 in range(4):
            pz = list(ZBANKS[b4])
            r_pz = list(ZRBANKS[b4])
            for half in range(2):
                em("vector", lambda e, half=half, b4=b4, pz=pz: e.tensor_add(
                    zb_t[b4][:, 512 * half:512 * half + 512], pz[half][:],
                    boutb[:, 512 * half:512 * half + 512]),
                   reads=[r_pz[half], r_boutb], writes=[r_zb[b4]])
            em("scalar", lambda e, b4=b4: e.activation(junkF4[b4], zb_t[b4], AF.Square,
                                                       accum_out=stc[:, 16 + b4:17 + b4]),
               reads=[r_zb[b4]], writes=[r_junkq[b4], r_stq[b4]])

    def emit_tail_rest(s):
        S0 = ST * s
        h = s % 2
        for b4 in range(4):
            blk = 4 * s + b4
            em("sync", lambda e, blk=blk, b4=b4: e.dma_start(
                out=y_sb[b4], in_=yscr_d[128 * blk:128 * blk + 128, :]),
               reads=[r_yscr[blk]], writes=[r_y[b4]], dma="yr%d" % (b4 % 2))
        em("vector", lambda e: e.tensor_scalar(stc[:, 4:8], stc[:, 16:20], 1.0 / D, RMS_EPS,
                                               ALU.mult, ALU.add),
           reads=r_stq, writes=[r_stc])
        rsqrt_batch()
        for b4 in range(4):
            em("scalar", lambda e, b4=b4: e.activation(zb_t[b4], zb_t[b4], AF.Identity,
                                                       scale=stc[:, 24 + b4:25 + b4]),
               reads=[r_stc, r_zb[b4]], writes=[r_zb[b4]])
            em("vector", lambda e, b4=b4: e.tensor_mul(zb_t[b4], zb_t[b4], goutb[:]),
               reads=[r_goutb, r_zb[b4]], writes=[r_zb[b4]])
        for b4 in range(4):
            blk = 4 * s + b4
            em("vector", lambda e, blk=blk, h=h, b4=b4: e.tensor_scalar(
                xnb_t[b4], x4[h][b4], lnmu[:, blk:blk + 1], lnis[:, blk:blk + 1],
                ALU.subtract, ALU.mult),
               reads=[r_x4[h][b4], r_lnmu, r_lnis], writes=[r_xnb[b4]])
            em("vector", lambda e, b4=b4: e.scalar_tensor_tensor(
                yb_t[b4], y_sb[b4], 1.0, xnb_t[b4], ALU.mult, ALU.add,
                accum_out=stc[:, 28 + b4:29 + b4]),
               reads=[r_y[b4], r_xnb[b4]], writes=[r_yb[b4], r_sty[b4]])
            em("scalar", lambda e, b4=b4: e.activation(junkF4[b4], yb_t[b4], AF.Square,
                                                       accum_out=stc[:, 16 + b4:17 + b4]),
               reads=[r_yb[b4]], writes=[r_junkq[b4], r_stq[b4]])
        em("vector", lambda e: e.tensor_scalar_mul(stc[:, 32:36], stc[:, 28:32], 1.0 / D),
           reads=r_sty, writes=[r_stc])
        em("vector", lambda e: e.tensor_scalar(stc[:, 4:8], stc[:, 16:20], 1.0 / D, FFN_LN_EPS,
                                               ALU.mult, ALU.add),
           reads=r_stq, writes=[r_stc])
        em("vector", lambda e: e.tensor_mul(stc[:, 20:24], stc[:, 32:36], stc[:, 32:36]),
           reads=[r_stc], writes=[r_stc])
        em("vector", lambda e: e.tensor_sub(stc[:, 4:8], stc[:, 4:8], stc[:, 20:24]),
           reads=[r_stc], writes=[r_stc])
        rsqrt_batch()
        em("vector", lambda e: e.scalar_tensor_tensor(
            stc[:, 36:40], stc[:, 32:36], -1.0, stc[:, 24:28], ALU.mult, ALU.mult),
           reads=[r_stc], writes=[r_stc])
        for b4 in range(4):
            em("scalar", lambda e, b4=b4: e.activation(
                yb_t[b4], yb_t[b4], AF.Identity,
                scale=stc[:, 24 + b4:25 + b4], bias=stc[:, 36 + b4:37 + b4]),
               reads=[r_stc, r_yb[b4]], writes=[r_yb[b4]])
            em("vector", lambda e, b4=b4: e.tensor_mul(yb_t[b4], yb_t[b4], gammab[:]),
               reads=[r_gammab, r_yb[b4]], writes=[r_yb[b4]])
            em("vector", lambda e, b4=b4: e.tensor_add(yb_t[b4], yb_t[b4], betab[:]),
               reads=[r_betab, r_yb[b4]], writes=[r_yb[b4]])
        for b4 in range(4):
            em("vector", lambda e, b4=b4, h=h: e.tensor_add(f_(b4), x4[h][b4], zb_t[b4]),
               reads=[r_x4[h][b4], r_zb[b4]], writes=[rF[b4]])
        for b4 in range(4):
            blk = 4 * s + b4
            em("vector", lambda e, b4=b4: e.tensor_add(f_(b4), f_(b4), yb_t[b4]),
               reads=[r_yb[b4], rF[b4]], writes=[rF[b4]])
            em("sync", lambda e, blk=blk, b4=b4: e.dma_start(
                out=out_d[128 * blk:128 * blk + 128, :], in_=f_(b4)),
               reads=[rF[b4]], dma="o%d" % (b4 % 2))

    emit_tail_front(0)
    emit_tail_z_mms(0)
    emit_tail_z_evac(0)
    for s in range(NS):
        if s + 1 < NS:
            emit_tail_front(s + 1)
            emit_tail_z_mms(s + 1)
        emit_tail_rest(s)
        if s + 1 < NS:
            emit_tail_z_evac(s + 1)
    for stream in ("o0", "o1"):
        key = "syncd_" + stream
        if key in b.cnt:
            b.prog["sync"].append(("wait", key, b.cnt[key]))

    sems = {}
    for name in sorted(b.cnt):
        sems[name] = EN(nc.semaphore(f"s_{name}"))
    blk_ctx = EN(nc.Block())

    def replay(eng_name):
        def run(eng):
            for item in b.prog[eng_name]:
                if item[0] == "wait":
                    eng.wait_ge(sems[item[1]], item[2])
                elif item[0] == "drain":
                    eng.drain()
                else:
                    _, fn, sem, inc = item
                    fn(eng).then_inc(sems[sem], inc)
        return run

    with nc.allow_low_precision(reason="bf16 matmul operand staging; accumulation is fp32 PSUM"):
        blk_ctx.sync(replay("sync"))
        blk_ctx.scalar(replay("scalar"))
        blk_ctx.vector(replay("vector"))
        blk_ctx.tensor(replay("tensor"))
        blk_ctx.gpsimd(replay("gpsimd"))
    ctx.close()
    return nc


_PERM64 = np.concatenate([np.arange(0, 64, 2), np.arange(1, 64, 2)])


def _perm_d():
    return np.concatenate([64 * h + _PERM64 for h in range(NH)])


def kernel(L0_attn_x, L0_attn_Wqkv, L0_attn_bqkv, L0_attn_gq, L0_attn_gk, L0_attn_Wout,
           L0_attn_bout, L0_attn_gout, L0_ffn_W_router, L0_ffn_W1, L0_ffn_b1, L0_ffn_Wo,
           L0_ffn_bo, L0_ffn_ln_gamma, L0_ffn_ln_beta):
    x = np.ascontiguousarray(np.asarray(L0_attn_x, dtype=np.float32))
    perm = _perm_d()
    wqkv = np.asarray(L0_attn_Wqkv, dtype=np.float32)
    wqkv_p = wqkv.copy()
    wqkv_p[0:D] = wqkv[0:D][perm]
    wqkv_p[D:2 * D] = wqkv[D:2 * D][perm]
    wqkvT = np.ascontiguousarray(wqkv_p.T)
    bqkv = np.asarray(L0_attn_bqkv, dtype=np.float32).copy()
    bqkv[0:D] = bqkv[0:D][perm]
    bqkv[D:2 * D] = bqkv[D:2 * D][perm]
    bqkv = bqkv.reshape(1, 3 * D)
    gq = np.ascontiguousarray(np.asarray(L0_attn_gq, dtype=np.float32)[perm]).reshape(1, D)
    gk = np.ascontiguousarray(np.asarray(L0_attn_gk, dtype=np.float32)[perm]).reshape(1, D)
    woutT = np.ascontiguousarray(np.asarray(L0_attn_Wout, dtype=np.float32).T)
    bout = np.asarray(L0_attn_bout, dtype=np.float32).reshape(1, D)
    gout = np.asarray(L0_attn_gout, dtype=np.float32).reshape(1, D)
    wr = np.ascontiguousarray(np.asarray(L0_ffn_W_router, dtype=np.float32))
    w1 = np.asarray(L0_ffn_W1, dtype=np.float32)
    w1 = np.ascontiguousarray(
        w1.reshape(E, 8, 128, 32, 128).transpose(0, 3, 2, 1, 4).reshape(E * 32, 128, 1024))
    b1t = np.ascontiguousarray(
        np.asarray(L0_ffn_b1, dtype=np.float32).reshape(E, 32, 128).transpose(0, 2, 1))
    wo = np.ascontiguousarray(np.asarray(L0_ffn_Wo, dtype=np.float32))
    bo = np.asarray(L0_ffn_bo, dtype=np.float32).reshape(1, D)
    gamma = np.asarray(L0_ffn_ln_gamma, dtype=np.float32).reshape(1, D)
    beta = np.asarray(L0_ffn_ln_beta, dtype=np.float32).reshape(1, D)

    nc = build_nc()
    shared = dict(wqkvT=wqkvT, bqkv=bqkv, gq=gq, gk=gk, wr=wr, woutT=woutT, bout=bout,
                  gout=gout, w1=w1, b1t=b1t, wo=wo, bo=bo, gamma=gamma, beta=beta)
    in_maps = [dict(x=np.ascontiguousarray(x[c]), **shared) for c in range(8)]
    res = run_bass_kernel_spmd(nc, in_maps, core_ids=list(range(8)))
    out = np.stack([res.results[c]["out"] for c in range(8)], axis=0)
    return out.astype(np.float32)
